# version 03
# baseline (speedup 1.0000x reference)
import contextlib
import dataclasses

import numpy as np

import concourse.bass as bass
import concourse.mybir as mybir
from concourse.bass_utils import run_bass_kernel_spmd

F32 = mybir.dt.float32
BF16 = mybir.dt.bfloat16
I32 = mybir.dt.int32
AF = mybir.ActivationFunctionType
ALU = mybir.AluOpType
AX = mybir.AxisListType

PI = float(np.pi)
INV2PI = 1.0 / (2.0 * PI)

D = 1024
T = 2048
NB = 16
NT = 4
TBT = 4
NH = 32
DS = 64
HD = 64
NCH = 32
DI = 2048
DMLP = 2560
HFF = 4096
AUXC = 1056
MAINC = 8192
ZO, XO, BO, CO = 0, 2048, 4096, 6144


def _bc(ap2d, groups, off=0):
    return dataclasses.replace(
        ap2d, offset=ap2d.offset + off,
        ap=[list(ap2d.ap[0])] + [list(g) for g in groups])



class _EngW:
    __slots__ = ("_e", "last", "sem")

    def __init__(self, eng, sem=None):
        self._e = eng
        self.last = None
        self.sem = sem

    def __getattr__(self, k):
        f = getattr(self._e, k)
        if callable(f) and k not in ("wait_ge", "nop", "print", "dma_start",
                                     "collective_compute"):
            def g(*a, **kw):
                if self.sem is not None and self.sem.n > 0:
                    self._e.wait_ge(self.sem.h, self.sem.n)
                r = f(*a, **kw)
                self.last = r
                if self.sem is not None:
                    self.sem.n += 1
                    r.then_inc(self.sem.h, 1)
                return r
            return g
        return f

class Sem:
    def __init__(self, h):
        self.h = h
        self.n = 0
        self.maxwait = 0
        self.lvl = {}

    def inc(self, instr, k=1):
        self.n += k
        instr.then_inc(self.h, k)

    def wait(self, eng):
        if self.n > 0:
            e = eng._e if isinstance(eng, _EngW) else eng
            e.wait_ge(self.h, self.n)
            self.maxwait = max(self.maxwait, self.n)
            self.lvl[id(e)] = max(self.lvl.get(id(e), 0), self.n)


def build(debug=(), phases=('wprep', 'p0', 'p1', 'post', '2a', '2b'), p1cut=5, p1grp=3):
    nc = bass.Bass("TRN2", target_bir_lowering=False)

    def din(name, shape, dtype=F32):
        return nc.dram_tensor(name, shape, dtype, kind="ExternalInput").ap()

    def dsc(name, shape, dtype):
        kind = "ExternalOutput" if name in debug else "Internal"
        return nc.dram_tensor(name, shape, dtype, kind=kind).ap()

    x_in = din("x", [T, D])
    w_main = din("w_main", [D, MAINC])
    w_aux = din("w_aux", [D, AUXC])
    w_out = din("w_out", [DI, D])
    w_g = din("w_g", [D, DMLP])
    w_u = din("w_u", [D, DMLP])
    w_d = din("w_d", [DMLP, D])
    w_fct = din("w_fct", [D, HFF])
    w_projt = din("w_projt", [HFF, D])
    n1w_c = din("n1w", [128, D])
    n2w_c = din("n2w", [128, D])
    bnw_c = din("bnw", [128, DS], BF16)
    cnw_c = din("cnw", [128, DS], BF16)
    bbias_c = din("bbias", [128, DI], BF16)
    cbias_c = din("cbias", [128, DI], BF16)
    dparam_c = din("dparam", [128, NH])
    dtbias_tm_c = din("dtbias_tm", [128, NH])
    dtbias_pk_c = din("dtbias_pk", [128, 1])
    nalog_pk_c = din("nalog_pk", [128, 1])
    alog_row_c = din("alog_row", [1, NH])
    fmc_c = din("fmc", [128, 3 * 512])
    shiftsum_c = din("shiftsum", [128, 128])
    lnw_c = din("lnw_c", [128, 32])
    lnb_c = din("lnb16_c", [128, 32], BF16)
    tri_c = din("tri_c", [128, 128])
    sel_c = din("sel_c", [128, 16 * 128])
    mask_c = din("mask_c", [128, 1])
    ident_c = din("identf_c", [128, 128])

    out_hbm = nc.dram_tensor("out", [T, D], F32, kind="ExternalOutput").ap()

    sp_win = dsc("sp_win", [D, MAINC], BF16)
    sp_wout = dsc("sp_wout", [DI, D], BF16)
    sp_wg = dsc("sp_wg", [D, DMLP], BF16)
    sp_wu = dsc("sp_wu", [D, DMLP], BF16)
    sp_wd = dsc("sp_wd", [DMLP, D], BF16)
    sp_wqf = dsc("sp_wqf", [D, HFF], BF16)
    sp_wqp = dsc("sp_wqp", [HFF, D], BF16)
    sp_h1 = dsc("sp_h1", [T, D], BF16)
    sp_h1lo = dsc("sp_h1lo", [T, D], BF16)
    sp_fm3 = dsc("sp_fm3", [128, 3 * 512], BF16)
    sp_dec = dsc("sp_dec", [128, 8], F32)
    sp_pc = dsc("sp_pc", [128, 8], F32)
    sp_xn16 = dsc("sp_xn16", [T, D], BF16)
    sp_xn = dsc("sp_xn", [T, D], F32)
    sp_s0 = dsc("sp_s0", [T, D], F32)
    sp_p0 = dsc("sp_p0", [T, D], F32)
    sp_cs = dsc("sp_cs", [T, 2 * D], BF16)
    sp_zs = dsc("sp_zs", [T, DI], BF16)
    sp_br = dsc("sp_br", [T, 2 * DI], BF16)
    sp_c1 = dsc("sp_c1", [T, 2 * DI], BF16)
    sp_c2 = dsc("sp_c2", [T, 2 * DI], BF16)
    sp_y1 = dsc("sp_y1", [T, DI], BF16)
    sp_fs = dsc("sp_fs", [NT, 128, 8 * 1024], F32)
    sp_y = dsc("sp_y", [T, DI], BF16)
    sp_sx = dsc("sp_sx", [T, D], F32)
    sp_h2 = dsc("sp_h2", [T, D], BF16)
    sp_gated = dsc("sp_gated", [T, DMLP], BF16)
    sp_acc = dsc("sp_acc", [T, D], F32)
    sp_hffn = dsc("sp_hffn", [T, HFF], BF16)
    cc1i = nc.dram_tensor("cc1i", [1, D], F32).ap()
    cc1o = nc.dram_tensor("cc1o", [2, D], F32).ap()
    cc2i = nc.dram_tensor("cc2i", [128, 1024], F32).ap()
    cc2o = nc.dram_tensor("cc2o", [2, 128, 1024], F32).ap()

    ctx = contextlib.ExitStack()
    with ctx:
        E = ctx.enter_context
        dts_tm = E(nc.sbuf_tensor("dts_tm", [128, NB * NH], F32))
        gam_tm = E(nc.sbuf_tensor("gam_tm", [128, NB * NH], F32))
        lsum_tm = E(nc.sbuf_tensor("lsum_tm", [128, NB * NH], F32))
        expa_tm = E(nc.sbuf_tensor("expa_tm", [128, NB * NH], F32))
        ds_tm = E(nc.sbuf_tensor("ds_tm", [128, NB * NH], F32))
        dec_bc = E(nc.sbuf_tensor("dec_bc", [128, 16 * NCH], F32))
        pc_bc = E(nc.sbuf_tensor("pc_bc", [128, 16 * NCH], F32))
        offrow = E(nc.sbuf_tensor("offrow", [1, D], F32))
        ccrecv = E(nc.sbuf_tensor("ccrecv", [1, D], F32))
        carry_rm = E(nc.sbuf_tensor("carry_rm", [128, 1024], F32))
        fslast = E(nc.sbuf_tensor("fslast", [128, 1024], F32))
        lnbw_row = E(nc.sbuf_tensor("lnbw_row", [1, D], F32))
        maskc = E(nc.sbuf_tensor("maskc", [128, 1], F32))
        identf = E(nc.sbuf_tensor("identf", [128, 128], F32))
        onescol = E(nc.sbuf_tensor("onescol", [128, 1], F32))
        ones32c = E(nc.sbuf_tensor("ones32c", [128, 32], F32))
        turncol = E(nc.sbuf_tensor("turncol", [128, 1], F32))
        turn32 = E(nc.sbuf_tensor("turn32", [128, 32], F32))
        onesrow = E(nc.sbuf_tensor("onesrow", [1, 128], F32))
        onecol = E(nc.sbuf_tensor("onecol", [128, 1], F32))
        halfpi = E(nc.sbuf_tensor("halfpi", [128, 1], F32))
        epsln = E(nc.sbuf_tensor("epsln", [128, 1], F32))
        thrq = E(nc.sbuf_tensor("thrq", [128, 1], F32))
        nthrq = E(nc.sbuf_tensor("nthrq", [128, 1], F32))
        sm = E(nc.sbuf_tensor("sm", [128, 16], F32))
        sm2 = E(nc.sbuf_tensor("sm2", [128, NH], F32))
        sm3 = E(nc.sbuf_tensor("sm3", [128, NH], F32))
        skip_t = E(nc.sbuf_tensor("skip_t", [128, TBT * NH], F32))
        pb = [E(nc.psum_tensor("pb%d" % i, [128, 512], F32)) for i in range(8)]

        dS = Sem(E(nc.semaphore("dS")))
        dG = Sem(E(nc.semaphore("dG")))
        dA = Sem(E(nc.semaphore("dA")))
        cP = Sem(E(nc.semaphore("cP")))
        cV = Sem(E(nc.semaphore("cV")))
        cA = Sem(E(nc.semaphore("cA")))
        ccs = Sem(E(nc.semaphore("ccs")))
        allsems = [dS, dG, dA, cP, cV, cA, ccs]

        blk = E(nc.Block())
        SY = _EngW(nc.sync)
        VE = _EngW(nc.vector, cV)
        SC = _EngW(nc.scalar, cA)
        PE = _EngW(nc.tensor)
        GP = _EngW(nc.gpsimd)

        def fence():
            for s in allsems:
                if s.n:
                    for eng in (SY, VE, SC, PE, GP):
                        eng.wait_ge(s.h, s.n)

        def dma(eng, sem, out, in_, **kw):
            e = eng._e if isinstance(eng, _EngW) else eng
            if sem.maxwait > sem.lvl.get(id(e), 0):
                e.wait_ge(sem.h, sem.maxwait)
                sem.lvl[id(e)] = sem.maxwait
            sem.inc(eng.dma_start(out=out, in_=in_, **kw), 16)

        def vsync():
            VE.wait_ge(cV.h, cV.n)

        dma(SY, dS, identf[:], ident_c)
        dma(SY, dS, maskc[:], mask_c)
        for _t, _v in [(onescol, 1.0), (ones32c, 1.0), (turncol, INV2PI),
                       (turn32, INV2PI), (onesrow, 1.0),
                       (onecol, 1.0), (halfpi, PI / 2.0), (epsln, 1e-5),
                       (offrow, 0.0), (fslast, 0.0)]:
            VE.memset(_t[:], _v)
        VE.wait_ge(cV.h, cV.n)

        if 'wprep' in phases:
            with contextlib.ExitStack() as _px:
                tA = _px.enter_context(nc.sbuf_tensor("wp_a", [128, HFF], F32))
                tB = _px.enter_context(nc.sbuf_tensor("wp_b", [128, HFF], F32))
                tS1 = _px.enter_context(nc.sbuf_tensor("wp_s1", [128, HFF], F32))
                tS2 = _px.enter_context(nc.sbuf_tensor("wp_s2", [128, HFF], F32))
                kQ = _px.enter_context(nc.sbuf_tensor("wp_q", [128, MAINC], BF16))
                kQ2 = _px.enter_context(nc.sbuf_tensor("wp_q2", [128, MAINC], BF16))
                qU = _px.enter_context(nc.sbuf_tensor("wp_qu", [128, 1024], BF16))
                lnw = _px.enter_context(nc.sbuf_tensor("wp_lnw", [128, 32], F32))
                lnb16 = _px.enter_context(nc.sbuf_tensor("wp_lnb", [128, 32], BF16))
                dma(SY, dS, lnw[:], lnw_c)
                dma(SY, dS, lnb16[:], lnb_c)

                for src, dst, nch, wd in [
                        (w_main, sp_win, 8, MAINC), (w_out, sp_wout, 16, D),
                        (w_g, sp_wg, 8, DMLP), (w_u, sp_wu, 8, DMLP),
                        (w_d, sp_wd, 20, D)]:
                    for ch in range(nch):
                        q = kQ if ch % 2 == 0 else kQ2
                        dA.wait(GP)
                        dma(GP, dG, q[:, 0:wd], src[ch * 128:(ch + 1) * 128, :])
                        dG.wait(SC)
                        dma(SC, dA, dst[ch * 128:(ch + 1) * 128, :], q[:, 0:wd])

                for (w_ap, q_dst, nch, wd, fold, do_lnbw) in [
                        (w_fct, sp_wqf, 8, HFF, False, False),
                        (w_projt, sp_wqp, 32, D, True, True)]:
                    dA.wait(VE)
                    VE.memset(sm[:, 0:1], 0.0)
                    VE.wait_ge(cV.h, cV.n)
                    for ch in range(nch):
                        buf = tA if ch % 2 == 0 else tB
                        SY.wait_ge(cV.h, cV.n)
                        dma(SY, dS, buf[:, 0:wd], w_ap[ch * 128:(ch + 1) * 128, :])
                        dS.wait(VE)
                        VE.tensor_reduce(sm[:, 1:2], buf[:, 0:wd], AX.X, ALU.add,
                                         apply_absolute_value=True)
                        VE.tensor_add(sm[:, 0:1], sm[:, 0:1], sm[:, 1:2])
                    PE.wait_ge(cV.h, cV.n)
                    PE.matmul(pb[2][0:1, 0:1], sm[:, 0:1], onescol[:], start=True, stop=True)
                    cP.inc(PE.last)
                    VE.wait_ge(cP.h, cP.n)
                    VE.tensor_scalar(sm[0:1, 2:3], pb[2][0:1, 0:1],
                                     1.0 / (nch * 128 * wd), 1e-6, ALU.mult, ALU.add)
                    VE.tensor_scalar(sm[0:1, 3:4], sm[0:1, 2:3], 0.5, None, ALU.mult)
                    PE.wait_ge(cV.h, cV.n)
                    PE.matmul(pb[2][:, 1:2], onesrow[:], sm[0:1, 3:4], start=True, stop=True)
                    cP.inc(PE.last)
                    VE.wait_ge(cP.h, cP.n)
                    VE.tensor_copy(thrq[:], pb[2][:, 1:2])
                    VE.tensor_scalar(nthrq[:], pb[2][:, 1:2], -1.0, None, ALU.mult)
                    for ch in range(nch):
                        buf = tA if ch % 2 == 0 else tB
                        q = kQ if ch % 2 == 0 else kQ2
                        SY.wait_ge(cA.h, cA.n)
                        dma(SY, dS, buf[:, 0:wd], w_ap[ch * 128:(ch + 1) * 128, :])
                        dS.wait(SC)
                        SC.wait_ge(cV.h, cV.n)
                        dA.wait(SC)
                        SC.activation(tS1[:, 0:wd], buf[:, 0:wd], AF.Sign, bias=nthrq[:])
                        SC.activation(tS2[:, 0:wd], buf[:, 0:wd], AF.Sign, bias=thrq[:])
                        VE.wait_ge(cA.h, cA.n)
                        VE.tensor_add(tS1[:, 0:wd], tS1[:, 0:wd], tS2[:, 0:wd])
                        if do_lnbw:
                            VE.tensor_copy(qU[:, 0:wd], tS1[:, 0:wd])
                        if fold:
                            VE.tensor_scalar(q[:, 0:wd], tS1[:, 0:wd], lnw[:, ch:ch + 1],
                                             None, ALU.mult)
                        else:
                            VE.tensor_copy(q[:, 0:wd], tS1[:, 0:wd])
                        if do_lnbw:
                            PE.wait_ge(cV.h, cV.n)
                            PE.matmul(pb[0][0:1, :], lnb16[:, ch:ch + 1], qU[:, 0:512],
                                      start=(ch == 0), stop=(ch == nch - 1))
                            PE.matmul(pb[1][0:1, :], lnb16[:, ch:ch + 1], qU[:, 512:1024],
                                      start=(ch == 0), stop=(ch == nch - 1))
                            cP.inc(PE.last)
                            SC.wait_ge(cP.h, cP.n)
                        else:
                            SC.wait_ge(cV.h, cV.n)
                        dma(SC, dA, q_dst[ch * 128:(ch + 1) * 128, :], q[:, 0:wd])
                    if do_lnbw:
                        VE.wait_ge(cP.h, cP.n)
                        VE.tensor_copy(lnbw_row[0:1, 0:512], pb[0][0:1, :])
                        VE.tensor_copy(lnbw_row[0:1, 512:1024], pb[1][0:1, :])
                fence()

        with nc.sbuf_tensor("hfm", [128, 8 * T], BF16) as hfm:

            if 'p0' in phases:
                with contextlib.ExitStack() as _px:
                    xa = _px.enter_context(nc.sbuf_tensor("p0_xa", [128, D], F32))
                    xb = _px.enter_context(nc.sbuf_tensor("p0_xb", [128, D], F32))
                    na = _px.enter_context(nc.sbuf_tensor("p0_na", [128, D], F32))
                    nb = _px.enter_context(nc.sbuf_tensor("p0_nb", [128, D], F32))
                    s0b = _px.enter_context(nc.sbuf_tensor("p0_s0", [128, D], F32))
                    sqb = _px.enter_context(nc.sbuf_tensor("p0_sq", [128, D], F32))
                    k1 = _px.enter_context(nc.sbuf_tensor("p0_k1", [128, D], BF16))
                    k2 = _px.enter_context(nc.sbuf_tensor("p0_k2", [128, D], BF16))
                    n1w = _px.enter_context(nc.sbuf_tensor("p0_n1w", [128, D], F32))
                    waux = _px.enter_context(nc.sbuf_tensor("p0_waux", [128, 8 * AUXC], BF16))
                    wauxlo = _px.enter_context(nc.sbuf_tensor("p0_wauxlo", [128, 8 * AUXC], BF16))
                    wauxf = _px.enter_context(nc.sbuf_tensor("p0_wauxf", [128, AUXC], F32))
                    hlo = _px.enter_context(nc.sbuf_tensor("p0_hlo", [128, 8 * 512], BF16))
                    h1f = _px.enter_context(nc.sbuf_tensor("p0_h1f", [128, D], F32))
                    k3 = _px.enter_context(nc.sbuf_tensor("p0_k3", [128, D], BF16))
                    lsx = _px.enter_context(nc.sbuf_tensor("p0_lsx", [128, NB * NH], F32))
                    alog_row = _px.enter_context(nc.sbuf_tensor("p0_alogr", [1, NH], F32))
                    pP = _px.enter_context(nc.sbuf_tensor("p0_pp", [128, D], F32))
                    f_dts = _px.enter_context(nc.sbuf_tensor("p0_f1", [128, 512], F32))
                    f_A = _px.enter_context(nc.sbuf_tensor("p0_f2", [128, 512], F32))
                    f_alpha = _px.enter_context(nc.sbuf_tensor("p0_f3", [128, 512], F32))
                    f_acum = _px.enter_context(nc.sbuf_tensor("p0_f4", [128, 512], F32))
                    f_t1 = _px.enter_context(nc.sbuf_tensor("p0_f5", [128, 512], F32))
                    f_t2 = _px.enter_context(nc.sbuf_tensor("p0_f6", [128, 512], F32))
                    f_out = _px.enter_context(nc.sbuf_tensor("p0_f7", [128, 512], F32))
                    fmc = _px.enter_context(nc.sbuf_tensor("p0_fc", [128, 3 * 512], F32))
                    onesf = _px.enter_context(nc.sbuf_tensor("p0_one", [128, 512], F32))
                    atot = _px.enter_context(nc.sbuf_tensor("p0_at", [128, 8], F32))
                    at2 = _px.enter_context(nc.sbuf_tensor("p0_at2", [128, 8], F32))
                    tri = _px.enter_context(nc.sbuf_tensor("p0_tri", [128, 128], F32))
                    shiftsum = _px.enter_context(nc.sbuf_tensor("p0_shs", [128, 128], F32))
                    dtbias_tm = _px.enter_context(nc.sbuf_tensor("p0_dtb", [128, NH], F32))
                    dtbias_pk = _px.enter_context(nc.sbuf_tensor("p0_dtbp", [128, 1], F32))
                    alog_pk = _px.enter_context(nc.sbuf_tensor("p0_alp", [128, 1], F32))
                    sinb = _px.enter_context(nc.sbuf_tensor("p0_sin", [128, D], BF16))
                    cosb = _px.enter_context(nc.sbuf_tensor("p0_cos", [128, D], BF16))
                    dma(SY, dS, n1w[:], n1w_c)
                    dma(SY, dS, tri[:], tri_c)
                    dma(SY, dS, shiftsum[:], shiftsum_c)
                    dma(SY, dS, dtbias_tm[:], dtbias_tm_c)
                    dma(SY, dS, dtbias_pk[:], dtbias_pk_c)
                    dma(SY, dS, alog_pk[:], nalog_pk_c)
                    dma(SY, dS, alog_row[:], alog_row_c)
                    dma(SY, dS, fmc[:], fmc_c)
                    for dc in range(8):
                        SY.wait_ge(cV.h, cV.n)
                        dma(SY, dS, wauxf[:], w_aux[dc * 128:(dc + 1) * 128, :])
                        dS.wait(VE)
                        VE.tensor_copy(waux[:, dc * AUXC:(dc + 1) * AUXC], wauxf[:])
                        VE.tensor_sub(wauxf[:], wauxf[:],
                                      waux[:, dc * AUXC:(dc + 1) * AUXC])
                        VE.tensor_copy(wauxlo[:, dc * AUXC:(dc + 1) * AUXC], wauxf[:])
                        dS.wait(SY)
                    dS.wait(VE)
                    VE.memset(onesf[:], 1.0)
                    VE.wait_ge(cV.h, cV.n)

                    for tb in range(NB):
                        xB = xa if tb % 2 == 0 else xb
                        nB = na if tb % 2 == 0 else nb
                        sl = slice(tb * 128, (tb + 1) * 128)
                        dS.wait(SY)
                        SY.wait_ge(cV.h, cV.n)
                        dma(SY, dS, xB[:, :], x_in[sl, :])
                        dS.wait(SC)
                        SC.activation(sqb[:], xB[:], AF.Square, accum_out=sm[:, 0:1])
                        VE.wait_ge(cA.h, cA.n)
                        VE.tensor_reduce(sm[:, 1:2], xB[:], AX.X, ALU.add)
                        VE.tensor_scalar(sm[:, 2:3], sm[:, 1:2], 1.0 / D, None, ALU.mult)
                        VE.tensor_mul(sm[:, 3:4], sm[:, 2:3], sm[:, 2:3])
                        VE.scalar_tensor_tensor(sm[:, 4:5], sm[:, 0:1], 1.0 / D, sm[:, 3:4],
                                                ALU.mult, ALU.subtract)
                        SC.wait_ge(cV.h, cV.n)
                        SC.activation(sm[:, 5:6], sm[:, 4:5], AF.Sqrt, bias=epsln[:])
                        VE.wait_ge(cA.h, cA.n)
                        VE.reciprocal(sm[:, 6:7], sm[:, 5:6])
                        VE.tensor_scalar(nB[:], xB[:], sm[:, 2:3], sm[:, 6:7],
                                         ALU.subtract, ALU.mult)
                        VE.tensor_add(s0b[:], xB[:], nB[:])
                        SC.wait_ge(cV.h, cV.n)
                        SC.activation(k1[:], nB[:], AF.Copy)
                        SC.activation(sqb[:], nB[:], AF.Square, accum_out=sm[:, 7:8])
                        VE.wait_ge(cA.h, cA.n)
                        VE.tensor_scalar(sm[:, 8:9], sm[:, 7:8], 1.0 / D, 1e-5,
                                         ALU.mult, ALU.add)
                        SC.wait_ge(cV.h, cV.n)
                        SC.activation(sm[:, 9:10], sm[:, 8:9], AF.Sqrt)
                        VE.wait_ge(cA.h, cA.n)
                        VE.reciprocal(sm[:, 10:11], sm[:, 9:10])
                        VE.scalar_tensor_tensor(h1f[:], nB[:], sm[:, 10:11], n1w[:],
                                                ALU.mult, ALU.mult)
                        VE.tensor_copy(k2[:], h1f[:])
                        VE.tensor_sub(h1f[:], h1f[:], k2[:])
                        VE.tensor_copy(k3[:], h1f[:])
                        SY.wait_ge(cV.h, cV.n)
                        dma(SY, dS, sp_xn[sl, :], nB[:])
                        dma(SY, dS, sp_s0[sl, :], s0b[:])
                        dma(SY, dS, sp_xn16[sl, :], k1[:])
                        dma(SY, dS, sp_h1[sl, :], k2[:])
                        dma(SY, dS, sp_h1lo[sl, :], k3[:])

                    dS.wait(SY)
                    for dc in range(8):
                        dma(SY, dS, hfm[:, dc * T:(dc + 1) * T],
                            sp_h1[:, dc * 128:(dc + 1) * 128], transpose=True)

                    dS.wait(PE)
                    dG.wait(PE)
                    for q in range(4):
                        SY.wait_ge(cP.h, cP.n)
                        for dc in range(8):
                            dma(SY, dS, hlo[:, dc * 512:(dc + 1) * 512],
                                sp_h1lo[q * 512:(q + 1) * 512, dc * 128:(dc + 1) * 128],
                                transpose=True)
                        dS.wait(PE)
                        for dc in range(8):
                            for wop, hop, fst, lst in (
                                    (waux, hfm[:, dc * T + q * 512:dc * T + (q + 1) * 512],
                                     dc == 0, False),
                                    (wauxlo, hfm[:, dc * T + q * 512:dc * T + (q + 1) * 512],
                                     False, False),
                                    (waux, hlo[:, dc * 512:(dc + 1) * 512],
                                     False, dc == 7)):
                                PE.matmul(pb[3][q * 32:(q + 1) * 32, :],
                                          wop[:, dc * AUXC:dc * AUXC + NH],
                                          hop, start=fst, stop=lst,
                                          tile_position=(0, q * 32))
                        cP.inc(PE.last)
                    VE.wait_ge(cP.h, cP.n)
                    VE.tensor_copy(f_t1[:], pb[3][:, :])
                    SC.wait_ge(cV.h, cV.n)
                    SC.activation(f_t2[:], f_t1[:], AF.Exp, bias=dtbias_pk[:])
                    SC.wait_ge(cA.h, cA.n)
                    SC.activation(f_dts[:], f_t2[:], AF.Ln, bias=onecol[:])
                    SC.wait_ge(cA.h, cA.n)
                    SC.activation(sm[:, 11:12], alog_pk[:], AF.Exp)
                    VE.wait_ge(cA.h, cA.n)
                    VE.tensor_scalar(sm[:, 12:13], sm[:, 11:12], -1.0, None, ALU.mult)
                    VE.tensor_scalar(f_A[:], f_dts[:], sm[:, 12:13], None, ALU.mult)
                    SC.wait_ge(cV.h, cV.n)
                    SC.activation(f_alpha[:], f_A[:], AF.Exp)
                    VE.wait_ge(cA.h, cA.n)
                    VE.tensor_tensor_scan(f_acum[:], fmc[:, 0:512], f_A[:], 0.0,
                                          ALU.mult, ALU.add)
                    VE.tensor_mul(f_t1[:], f_alpha[:], fmc[:, 0:512])
                    VE.tensor_tensor_scan(f_t2[:], f_t1[:], onesf[:], 0.0, ALU.mult, ALU.add)
                    VE.tensor_add(f_out[:], f_t2[:], fmc[:, 512:1024])
                    SY.wait_ge(cV.h, cV.n)
                    VE.tensor_copy(f_t1.ap().bitcast(BF16)[:, 0:512], f_out[:])
                    SY.wait_ge(cV.h, cV.n)
                    dma(SY, dS, sp_fm3[:, 0:512], f_t1.ap().bitcast(BF16)[:, 0:512])
                    SC.wait_ge(cV.h, cV.n)
                    SC.activation(f_out[:], f_acum[:], AF.Exp)
                    VE.wait_ge(cA.h, cA.n)
                    dS.wait(VE)
                    VE.tensor_copy(f_t1.ap().bitcast(BF16)[:, 0:512], f_out[:])
                    SY.wait_ge(cV.h, cV.n)
                    dma(SY, dS, sp_fm3[:, 512:1024], f_t1.ap().bitcast(BF16)[:, 0:512])
                    VE.scalar_tensor_tensor(f_t2[:], f_acum[:], -1.0,
                                            _bc(f_acum[:, 0:1], [(64, 8), (0, 64)], off=63),
                                            ALU.mult, ALU.add)
                    SC.wait_ge(cV.h, cV.n)
                    SC.activation(f_out[:], f_t2[:], AF.Exp)
                    VE.wait_ge(cA.h, cA.n)
                    dS.wait(VE)
                    VE.tensor_copy(f_t1.ap().bitcast(BF16)[:, 0:512], f_out[:])
                    SY.wait_ge(cV.h, cV.n)
                    dma(SY, dS, sp_fm3[:, 1024:1536], f_t1.ap().bitcast(BF16)[:, 0:512])
                    VE.tensor_copy(atot[:], _bc(f_acum[:, 0:1], [(64, 8)], off=63))
                    VE.tensor_tensor_scan(at2[:], atot[:], onesf[:, 0:8], 0.0,
                                          ALU.add, ALU.mult)
                    PE.wait_ge(cV.h, cV.n)
                    PE.matmul(pb[6][:, 0:1], shiftsum[:], at2[:, 7:8], start=True, stop=True)
                    cP.inc(PE.last)
                    VE.wait_ge(cP.h, cP.n)
                    VE.tensor_scalar(at2[:], at2[:], pb[6][:, 0:1], None, ALU.add)
                    SC.wait_ge(cV.h, cV.n)
                    dS.wait(SC)
                    SC.activation(f_t1[:, 0:8], atot[:], AF.Exp)
                    SC.activation(f_t2[:, 0:8], at2[:], AF.Exp)
                    SY.wait_ge(cA.h, cA.n)
                    dma(SY, dS, sp_dec, f_t1[:, 0:8])
                    dma(SY, dS, sp_pc, f_t2[:, 0:8])
                    dS.wait(SY)
                    for i3, dst in enumerate((lsum_tm, expa_tm, ds_tm)):
                        for r in range(4):
                            dma(SY, dS, f_t1.ap().bitcast(BF16)[:, 0:128],
                                sp_fm3[:, i3 * 512 + r * 128:i3 * 512 + (r + 1) * 128],
                                transpose=True)
                            dS.wait(VE)
                            for q in range(4):
                                tb = q * 4 + r
                                VE.tensor_copy(dst[:, tb * NH:(tb + 1) * NH],
                                               f_t1.ap().bitcast(BF16)[:, q * 32:(q + 1) * 32])
                            SY.wait_ge(cV.h, cV.n)
                    import dataclasses as _dc
                    for i3, (sp_src, dst) in enumerate(((sp_dec, dec_bc), (sp_pc, pc_bc))):
                        for par in range(2):
                            srcap = _dc.replace(
                                sp_src, offset=sp_src.offset + par * 8,
                                ap=[[16, 1], [16, 16], [256, 4], [1, 8]])
                            dstap = _dc.replace(
                                f_t2[0:1, 0:512],
                                ap=[list(f_t2[0:1, 0:512].ap[0]),
                                    [32, 16], [8, 4], [1, 8]])
                            SY.wait_ge(cP.h, cP.n)
                            dma(SY, dS, dstap, srcap)
                            dS.wait(PE)
                            PE.wait_ge(cV.h, cV.n)
                            PE.matmul(pb[6 + par][:, :], onesrow[:], f_t2[0:1, 0:512],
                                      start=True, stop=True)
                            cP.inc(PE.last)
                            VE.wait_ge(cP.h, cP.n)
                            VE.tensor_copy(dst[par * 64:(par + 1) * 64, :],
                                           pb[6 + par][par * 64:(par + 1) * 64, :])
                    PE.wait_ge(cV.h, cV.n)

                    PE.wait_ge(cV.h, cV.n)

                    for tb in range(NB):
                        sl = slice(tb * 128, (tb + 1) * 128)
                        SY.wait_ge(cP.h, cP.n)
                        for dc in range(8):
                            dma(SY, dS, hlo[:, dc * 512:dc * 512 + 128],
                                sp_h1lo[sl, dc * 128:(dc + 1) * 128], transpose=True)
                        dS.wait(PE)
                        PE.wait_ge(cV.h, cV.n)
                        PE.wait_ge(cA.h, cA.n)
                        for si, (c0, c1) in enumerate([(0, 512), (512, 1024), (1024, 1056)]):
                            for dc in range(8):
                                for hop, wop, fst, lst in (
                                        (hfm[:, dc * T + tb * 128:dc * T + (tb + 1) * 128],
                                         waux, dc == 0, False),
                                        (hfm[:, dc * T + tb * 128:dc * T + (tb + 1) * 128],
                                         wauxlo, False, False),
                                        (hlo[:, dc * 512:dc * 512 + 128],
                                         waux, False, dc == 7)):
                                    PE.matmul(pb[si][:, 0:c1 - c0], hop,
                                              wop[:, dc * AUXC + c0:dc * AUXC + c1],
                                              start=fst, stop=lst)
                        cP.inc(PE.last)
                        VE.wait_ge(cP.h, cP.n)
                        VE.tensor_add(sm2[:], pb[0][:, 0:NH], dtbias_tm[:])
                        SC.wait_ge(cV.h, cV.n)
                        SC.activation(sm3[:], sm2[:], AF.Exp)
                        SC.wait_ge(cA.h, cA.n)
                        SC.activation(dts_tm[:, tb * NH:(tb + 1) * NH], sm3[:], AF.Ln,
                                      bias=onecol[:])
                        VE.wait_ge(cA.h, cA.n)
                        dsl = dts_tm[:, tb * NH:(tb + 1) * NH]
                        VE.tensor_tensor(pP[:, 0:480], pb[0][:, 32:512],
                                         _bc(dsl[:, 0:1], [(1, 15), (0, 32)]), ALU.mult)
                        VE.tensor_tensor(pP[:, 480:992], pb[1][:, 0:512],
                                         _bc(dsl[:, 0:1], [(1, 16), (0, 32)], off=15), ALU.mult)
                        VE.tensor_tensor(pP[:, 992:1024], pb[2][:, 0:32],
                                         _bc(dsl[:, 0:1], [(1, 1), (0, 32)], off=31), ALU.mult)
                        PE.wait_ge(cV.h, cV.n)
                        PE.matmul(pb[4][0:NH, :], turn32[:], pP[:, 0:512],
                                  start=(tb == 0), stop=(tb == NB - 1))
                        PE.matmul(pb[5][0:NH, :], turn32[:], pP[:, 512:1024],
                                  start=(tb == 0), stop=(tb == NB - 1))
                        cP.inc(PE.last)
                        SY.wait_ge(cV.h, cV.n)
                        dma(SY, dS, sp_p0[sl, :], pP[:])
                        dS.wait(SY)
                        dS.wait(VE)

                    SC.activation(sm2[0:1, :], alog_row[0:1, :], AF.Exp)
                    PE.wait_ge(cA.h, cA.n)
                    PE.matmul(pb[3][:, 0:NH], onesrow[:], sm2[0:1, :], start=True, stop=True)
                    cP.inc(PE.last)
                    VE.wait_ge(cP.h, cP.n)
                    VE.tensor_scalar(sm3[:], pb[3][:, 0:NH], -1.0, None, ALU.mult)
                    for tb4 in range(NB):
                        gsl = slice(tb4 * NH, (tb4 + 1) * NH)
                        VE.tensor_mul(gam_tm[:, gsl], dts_tm[:, gsl], sm3[:])
                    SC.wait_ge(cV.h, cV.n)
                    SC.activation(lsx[:], gam_tm[:], AF.Exp)
                    VE.wait_ge(cA.h, cA.n)
                    VE.tensor_scalar(f_t1[:], gam_tm[:], 1e-6, None, ALU.add)
                    VE.reciprocal(f_t2[:], f_t1[:])
                    VE.tensor_scalar(f_t1[:], lsx[:], 1.0, None, ALU.subtract)
                    VE.tensor_mul(lsx[:], f_t1[:], f_t2[:])
                    VE.tensor_scalar(gam_tm[:], lsx[:], 0.5, 1.0, ALU.mult, ALU.add)

                    VE.wait_ge(cP.h, cP.n)
                    VE.tensor_copy(ccrecv[0:1, 0:512], pb[4][0:1, :])
                    VE.tensor_copy(ccrecv[0:1, 512:1024], pb[5][0:1, :])
                    GP.wait_ge(cV.h, cV.n)
                    dma(GP, dG, cc1i, ccrecv[:])
                    dG.wait(GP)
                    i = GP.collective_compute(
                        "AllGather", ALU.bypass,
                        replica_groups=[[0, 1], [2, 3], [4, 5], [6, 7]],
                        ins=[cc1i.opt()], outs=[cc1o.opt()])
                    ccs.inc(i)
                    GP.wait_ge(ccs.h, ccs.n)
                    dma(GP, dG, ccrecv[:], cc1o[0:1, :])
                    dG.wait(VE)
                    VE.tensor_scalar(offrow[:], ccrecv[:], maskc[0:1, 0:1], None, ALU.mult)

                    rR = xa
                    kF = xb
                    kI = na[:].bitcast(I32)
                    for tb in range(NB):
                        sl = slice(tb * 128, (tb + 1) * 128)
                        SY.wait_ge(cV.h, cV.n)
                        dma(SY, dS, pP[:], sp_p0[sl, :])
                        dS.wait(PE)
                        PE.wait_ge(cV.h, cV.n)
                        for cs in range(2):
                            bank = pb[4 + cs]
                            PE.matmul(bank[:, :], tri[:], pP[:, cs * 512:(cs + 1) * 512],
                                      start=True, stop=False)
                            PE.matmul(bank[:, :], onesrow[:],
                                      offrow[0:1, cs * 512:(cs + 1) * 512],
                                      start=False, stop=True)
                            PE.matmul(pb[6 + cs][0:NH, :], turn32[:],
                                      pP[:, cs * 512:(cs + 1) * 512], start=True, stop=True)
                        cP.inc(PE.last)
                        VE.wait_ge(cP.h, cP.n)
                        for cs in range(2):
                            c0 = cs * 512
                            VE.tensor_add(offrow[0:1, c0:c0 + 512],
                                          offrow[0:1, c0:c0 + 512], pb[6 + cs][0:1, :])
                        def wrap_turns(c0, src_ap, shift):
                            c1 = c0 + 512
                            VE.wait_ge(cA.h, cA.n)
                            if shift:
                                VE.tensor_scalar(kI[:, c0:c1], src_ap, shift, None, ALU.add)
                                VE.tensor_scalar(rR[:, c0:c1], src_ap, shift, None, ALU.add)
                            else:
                                VE.tensor_scalar(kI[:, c0:c1], src_ap, 1.0, None, ALU.mult)
                                VE.tensor_copy(rR[:, c0:c1], src_ap)
                            VE.tensor_copy(kF[:, c0:c1], kI[:, c0:c1])
                            VE.tensor_sub(rR[:, c0:c1], rR[:, c0:c1], kF[:, c0:c1])
                            VE.tensor_scalar(kF[:, c0:c1], rR[:, c0:c1], 0.5, None, ALU.is_gt)
                            VE.tensor_sub(rR[:, c0:c1], rR[:, c0:c1], kF[:, c0:c1])
                            VE.tensor_scalar(kF[:, c0:c1], rR[:, c0:c1], -0.5, None, ALU.is_lt)
                            VE.tensor_add(rR[:, c0:c1], rR[:, c0:c1], kF[:, c0:c1])

                        for cs in range(2):
                            wrap_turns(cs * 512, pb[4 + cs][:, :], 0.0)
                        SC.wait_ge(cV.h, cV.n)
                        SC.activation(sinb[:], rR[:], AF.Sin, scale=2 * PI)
                        for cs in range(2):
                            wrap_turns(cs * 512, pb[4 + cs][:, :], 0.25)
                        SC.wait_ge(cV.h, cV.n)
                        SC.activation(cosb[:], rR[:], AF.Sin, scale=2 * PI)
                        SY.wait_ge(cA.h, cA.n)
                        dma(SY, dS, sp_cs[sl, 0:D], sinb[:])
                        dma(SY, dS, sp_cs[sl, D:2 * D], cosb[:])
                        dS.wait(VE)
                    fence()

            if 'p1' in phases:
                with contextlib.ExitStack() as _px:
                    wsA = _px.enter_context(nc.sbuf_tensor("p1_wsa", [128, 8 * 512], BF16))
                    wsB = _px.enter_context(nc.sbuf_tensor("p1_wsb", [128, 8 * 512], BF16))
                    u_t = _px.enter_context(nc.sbuf_tensor("p1_ut", [128, TBT * DI], BF16))
                    bds_a = _px.enter_context(nc.sbuf_tensor("p1_bdsa", [128, DI], BF16))
                    bds_b = _px.enter_context(nc.sbuf_tensor("p1_bdsb", [128, DI], BF16))
                    bhat = _px.enter_context(nc.sbuf_tensor("p1_bhat", [128, 32 * 128], BF16))
                    chat = _px.enter_context(nc.sbuf_tensor("p1_chat", [128, 32 * 128], BF16))
                    g2a = _px.enter_context(nc.sbuf_tensor("p1_g2a", [128, NH * 64], BF16))
                    g2b = _px.enter_context(nc.sbuf_tensor("p1_g2b", [128, NH * 64], BF16))
                    fs_sb = _px.enter_context(nc.sbuf_tensor("p1_fs", [128, 8 * 1024], F32))
                    csin = _px.enter_context(nc.sbuf_tensor("p1_sin", [128, TBT * D], BF16))
                    ccos = _px.enter_context(nc.sbuf_tensor("p1_cos", [128, TBT * D], BF16))
                    st1 = _px.enter_context(nc.sbuf_tensor("p1_st1", [128, 512], BF16))
                    st2 = _px.enter_context(nc.sbuf_tensor("p1_st2", [128, 512], BF16))
                    st3 = _px.enter_context(nc.sbuf_tensor("p1_st3", [128, 512], BF16))
                    bnw = _px.enter_context(nc.sbuf_tensor("p1_bnw", [128, DS], BF16))
                    cnw = _px.enter_context(nc.sbuf_tensor("p1_cnw", [128, DS], BF16))
                    bbias = _px.enter_context(nc.sbuf_tensor("p1_bb", [128, DI], BF16))
                    cbias = _px.enter_context(nc.sbuf_tensor("p1_cb", [128, DI], BF16))
                    dparam = _px.enter_context(nc.sbuf_tensor("p1_dp", [128, NH], F32))
                    sf1 = _px.enter_context(nc.sbuf_tensor("p1_sf1", [128, 512], F32))
                    VE.memset(u_t[:, 0:2 * DI], 0.0)
                    VE.memset(g2a[64:128, :], 0.0)
                    VE.memset(g2b[0:64, :], 0.0)
                    VE.memset(bds_a[64:128, :], 0.0)
                    VE.memset(bds_b[0:64, :], 0.0)
                    dS.wait(VE)
                    SY.wait_ge(cV.h, cV.n)
                    for sp in (sp_br, sp_c1, sp_c2):
                        for tb in range(NB):
                            dma(SY, dS, sp[tb * 128:(tb + 1) * 128, :], u_t[:, 0:2 * DI])
                    dS.wait(VE)
                    dma(SY, dS, bnw[:], bnw_c)
                    dma(SY, dS, cnw[:], cnw_c)
                    dma(SY, dS, bbias[:], bbias_c)
                    dma(SY, dS, cbias[:], cbias_c)
                    dma(SY, dS, dparam[:], dparam_c)

                    for ti in range(NT):
                        t0 = ti * 512
                        if p1grp >= 3:
                            SY.wait_ge(cV.h, cV.n)
                            for tb4 in range(TBT):
                                sl = slice(t0 + tb4 * 128, t0 + (tb4 + 1) * 128)
                                dma(SY, dS, csin[:, tb4 * D:(tb4 + 1) * D], sp_cs[sl, 0:D])
                                dma(SY, dS, ccos[:, tb4 * D:(tb4 + 1) * D], sp_cs[sl, D:2 * D])

                        for cs in range(16):
                            wbuf = wsA if cs % 2 == 0 else wsB
                            SY.wait_ge(cP.h, cP.n)
                            dma(SY, dS, wbuf.ap().rearrange("p (c n) -> p c n", c=8),
                                sp_win.rearrange("(c p) n -> p c n", p=128)
                                [:, :, cs * 512:(cs + 1) * 512])
                            gcol = cs * 512
                            for tb4 in range(TBT):
                                tb = ti * TBT + tb4
                                sl = slice(tb * 128, (tb + 1) * 128)
                                bank = pb[(cs * TBT + tb4) % 2]
                                dS.wait(PE)
                                PE.wait_ge(cV.h, cV.n)
                                PE.wait_ge(cA.h, cA.n)
                                dA.wait(PE)
                                for dc in range(8):
                                    PE.matmul(bank[:, :],
                                              hfm[:, dc * T + tb * 128:dc * T + (tb + 1) * 128],
                                              wbuf[:, dc * 512:(dc + 1) * 512],
                                              start=(dc == 0), stop=(dc == 7))
                                cP.inc(PE.last)
                                if gcol < DI:
                                    SC.wait_ge(cP.h, cP.n)
                                    dA.wait(SC)
                                    SC.wait_ge(cV.h, cV.n)
                                    SC.activation(st1[:, :], bank[:, :], AF.Silu)
                                    SC.wait_ge(cA.h, cA.n)
                                    dma(SC, dA, sp_zs[sl, gcol:gcol + 512], st1[:, :])
                                elif gcol < 2 * DI:
                                    lc = gcol - DI
                                    h0 = lc // 64
                                    VE.wait_ge(cP.h, cP.n)
                                    gsl = gam_tm[:, tb * NH + h0:tb * NH + h0 + 1]
                                    VE.tensor_tensor(
                                        u_t[:, tb4 * DI + lc:tb4 * DI + lc + 512],
                                        bank[:, :], _bc(gsl, [(1, 8), (0, 64)]), ALU.mult)
                                    VE.tensor_reduce(
                                        sm2[:, h0:h0 + 8],
                                        bank.ap().rearrange("p (h k) -> p h k", h=8),
                                        AX.X, ALU.add)
                                    VE.tensor_tensor(skip_t[:, tb4 * NH + h0:tb4 * NH + h0 + 8],
                                                     sm2[:, h0:h0 + 8],
                                                     dparam[:, h0:h0 + 8], ALU.mult)
                                else:
                                    isB = gcol < CO
                                    lc = gcol - (BO if isB else CO)
                                    h0 = lc // 64
                                    SC.wait_ge(cP.h, cP.n)
                                    for hh in range(8):
                                        SC.activation(sf1[:, hh * 64:(hh + 1) * 64],
                                                      bank[:, hh * 64:(hh + 1) * 64],
                                                      AF.Square,
                                                      accum_out=sm2[:, h0 + hh:h0 + hh + 1])
                                    VE.wait_ge(cA.h, cA.n)
                                    VE.tensor_scalar(sm3[:, h0:h0 + 8], sm2[:, h0:h0 + 8],
                                                     1.0 / DS, 1e-5, ALU.mult, ALU.add)
                                    SC.wait_ge(cV.h, cV.n)
                                    SC.activation(sm2[:, h0:h0 + 8], sm3[:, h0:h0 + 8], AF.Sqrt)
                                    VE.wait_ge(cA.h, cA.n)
                                    dA.wait(VE)
                                    VE.reciprocal(sm3[:, h0:h0 + 8], sm2[:, h0:h0 + 8])
                                    wt = bnw if isB else cnw
                                    bt = bbias if isB else cbias
                                    VE.tensor_tensor(st1[:, :], bank[:, :],
                                                     _bc(sm3[:, h0:h0 + 1], [(1, 8), (0, 64)]),
                                                     ALU.mult)
                                    VE.tensor_tensor(st1[:, :], st1[:, :],
                                                     _bc(wt[:, 0:1], [(0, 8), (1, 64)]),
                                                     ALU.mult)
                                    VE.tensor_add(st1[:, :], st1[:, :], bt[:, lc:lc + 512])
                                    ev = _bc(st1[:, 0:1], [(64, 8), (1, 32)])
                                    od = _bc(st1[:, 0:1], [(64, 8), (1, 32)], off=32)
                                    aS = _bc(csin[:, 0:1], [(32, 8), (1, 32)],
                                             off=tb4 * D + h0 * 32)
                                    aC = _bc(ccos[:, 0:1], [(32, 8), (1, 32)],
                                             off=tb4 * D + h0 * 32)
                                    rev = _bc(st2[:, 0:1], [(64, 8), (1, 32)])
                                    rod = _bc(st2[:, 0:1], [(64, 8), (1, 32)], off=32)
                                    m1 = _bc(st3[:, 0:1], [(64, 8), (1, 32)])
                                    m2_ = _bc(st3[:, 0:1], [(64, 8), (1, 32)], off=32)
                                    VE.tensor_tensor(m1, aC, ev, ALU.mult)
                                    VE.tensor_tensor(m2_, aS, od, ALU.mult)
                                    VE.tensor_tensor(rev, m1, m2_, ALU.subtract)
                                    VE.tensor_tensor(m1, aS, ev, ALU.mult)
                                    VE.tensor_tensor(m2_, aC, od, ALU.mult)
                                    VE.tensor_tensor(rod, m1, m2_, ALU.add)
                                    if isB:
                                        SY.wait_ge(cV.h, cV.n)
                                        for par in range(2):
                                            dma(SY, dS,
                                                _bc(sp_br[sl, (h0 + par) * 128 + par * 64:],
                                                    [(256, 4), (1, 64)]),
                                                _bc(st2[:, par * 64:], [(128, 4), (1, 64)]))
                                    else:
                                        VE.tensor_tensor(
                                            st3[:, :], st2[:, :],
                                            _bc(lsum_tm[:, tb * NH + h0:tb * NH + h0 + 1],
                                                [(1, 8), (0, 64)]), ALU.mult)
                                        SY.wait_ge(cV.h, cV.n)
                                        for par in range(2):
                                            dma(SY, dS,
                                                _bc(sp_c1[sl, (h0 + par) * 128 + par * 64:],
                                                    [(256, 4), (1, 64)]),
                                                _bc(st3[:, par * 64:], [(128, 4), (1, 64)]))
                                        dS.wait(VE)
                                        VE.tensor_tensor(
                                            st3[:, :], st2[:, :],
                                            _bc(expa_tm[:, tb * NH + h0:tb * NH + h0 + 1],
                                                [(1, 8), (0, 64)]), ALU.mult)
                                        SY.wait_ge(cV.h, cV.n)
                                        for par in range(2):
                                            dma(SY, dS,
                                                _bc(sp_c2[sl, (h0 + par) * 128 + par * 64:],
                                                    [(256, 4), (1, 64)]),
                                                _bc(st3[:, par * 64:], [(128, 4), (1, 64)]))
                                    dS.wait(VE)

                        for tb4 in range(TBT):
                            tb = ti * TBT + tb4
                            sl = slice(tb * 128, (tb + 1) * 128)
                            if p1cut < 2:
                                continue
                            SY.wait_ge(cP.h, cP.n)
                            dS.wait(SY)
                            dA.wait(SY)
                            for nch in range(32):
                                dma(SY, dS, bhat[:, nch * 128:(nch + 1) * 128],
                                    sp_br[sl, nch * 128:(nch + 1) * 128], transpose=True)
                                dma(SY, dS, chat[:, nch * 128:(nch + 1) * 128],
                                    sp_c1[sl, nch * 128:(nch + 1) * 128], transpose=True)
                            for par in range(2):
                                bdst = bds_a if par == 0 else bds_b
                                rsl = slice(par * 64, par * 64 + 64)
                                dma(SY, dS, bdst[rsl, 0:DI],
                                    _bc(sp_br[tb * 128 + par * 64:tb * 128 + par * 64 + 64,
                                              0:1],
                                        [(256, 16), (192, 2), (1, 64)]))
                            dS.wait(VE)
                            VE.wait_ge(cP.h, cP.n)
                            for par in range(2):
                                bdst = bds_a if par == 0 else bds_b
                                rsl = slice(par * 64, par * 64 + 64)
                                VE.tensor_tensor(
                                    bdst[rsl, :], bdst[rsl, :],
                                    _bc(ds_tm[rsl, tb * NH:tb * NH + 1],
                                        [(1, 32), (0, 64)]), ALU.mult)
                            dS.wait(PE)
                            for hg in range(4 if p1cut >= 3 else 0):
                                bank = pb[2 + hg % 2]
                                PE.wait_ge(cV.h, cV.n)
                                for hh in range(8):
                                    h = hg * 8 + hh
                                    for c2 in range(2):
                                        PE.matmul(
                                            bank[64 * c2:64 * c2 + 64, hh * 64:(hh + 1) * 64],
                                            bhat[:, h * 128 + c2 * 64:h * 128 + c2 * 64 + 64],
                                            chat[:, h * 128 + c2 * 64:h * 128 + c2 * 64 + 64],
                                            start=True, stop=True)
                                cP.inc(PE.last)
                                VE.wait_ge(cP.h, cP.n)
                                VE.tensor_copy(g2a[0:64, hg * 512:(hg + 1) * 512], bank[0:64, :])
                                VE.tensor_copy(g2b[64:128, hg * 512:(hg + 1) * 512],
                                               bank[64:128, :])
                            for hg in range(4 if p1cut >= 4 else 0):
                                ybank = pb[4 + hg % 2]
                                PE.wait_ge(cV.h, cV.n)
                                for hh in range(8):
                                    h = hg * 8 + hh
                                    for c2 in range(2):
                                        PE.matmul(
                                            ybank[64 * c2:64 * c2 + 64, hh * 64:(hh + 1) * 64],
                                            (g2a if c2 == 0 else g2b)[:, hg * 512 + hh * 64:
                                                                      hg * 512 + (hh + 1) * 64],
                                            u_t[:, tb4 * DI + h * 64:tb4 * DI + (h + 1) * 64],
                                            start=True, stop=True)
                                cP.inc(PE.last)
                                VE.wait_ge(cP.h, cP.n)
                                ssl = skip_t[:, tb4 * NH + hg * 8:tb4 * NH + hg * 8 + 1]
                                VE.tensor_tensor(st1[:, :], ybank[:, :],
                                                 _bc(ssl, [(1, 8), (0, 64)]), ALU.add)
                                SY.wait_ge(cV.h, cV.n)
                                dma(SY, dS, sp_y1[sl, hg * 512:(hg + 1) * 512], st1[:, :])
                                dS.wait(VE)
                            for c2 in range(2 if p1cut >= 5 else 0):
                                c = tb * 2 + c2
                                cit = tb4 * 2 + c2
                                PE.wait_ge(cV.h, cV.n)
                                for h in range(NH):
                                    g = h // 2
                                    sbank = pb[g // 8]
                                    PE.matmul(
                                        sbank[(h % 2) * 64:(h % 2) * 64 + 64,
                                              (g % 8) * 64:(g % 8 + 1) * 64],
                                        (bds_a if c2 == 0 else bds_b)
                                        [:, h * 64:(h + 1) * 64],
                                        u_t[:, tb4 * DI + h * 64:tb4 * DI + (h + 1) * 64],
                                        start=True, stop=True)
                                cP.inc(PE.last)
                                VE.wait_ge(cP.h, cP.n)
                                for g in range(16):
                                    sbank = pb[g // 8]
                                    prev = (fslast[:, g * 64:(g + 1) * 64] if cit == 0 else
                                            fs_sb[:, (cit - 1) * 1024 + g * 64:
                                                  (cit - 1) * 1024 + (g + 1) * 64])
                                    VE.scalar_tensor_tensor(
                                        fs_sb[:, cit * 1024 + g * 64:cit * 1024 + (g + 1) * 64],
                                        prev, dec_bc[:, g * NCH + c:g * NCH + c + 1],
                                        sbank[:, (g % 8) * 64:(g % 8 + 1) * 64],
                                        ALU.mult, ALU.add)
                        if p1cut < 5:
                            continue
                        VE.tensor_copy(fslast[:, :], fs_sb[:, 7 * 1024:8 * 1024])
                        SY.wait_ge(cV.h, cV.n)
                        dma(SY, dS, sp_fs[ti], fs_sb[:])
                        dS.wait(SY)
                        dS.wait(VE)

                    if p1cut >= 4:
                        GP.wait_ge(cV.h, cV.n)
                        dma(GP, dG, cc2i, fslast[:])
                        dG.wait(GP)
                        i = GP.collective_compute(
                            "AllGather", ALU.bypass,
                            replica_groups=[[0, 1], [2, 3], [4, 5], [6, 7]],
                            ins=[cc2i.opt()], outs=[cc2o.opt()])
                        ccs.inc(i)
                        GP.wait_ge(ccs.h, ccs.n)
                        dma(GP, dG, carry_rm[:], cc2o[0])
                        dG.wait(VE)
                        VE.tensor_scalar(carry_rm[:], carry_rm[:], maskc[:], None, ALU.mult)
                    fence()

        if 'post' in phases:
            with contextlib.ExitStack() as _px:
                woutr = _px.enter_context(nc.sbuf_tensor("pp_wout", [128, 16 * D], BF16))
                fsl = _px.enter_context(nc.sbuf_tensor("pp_fsl", [128, 8 * 1024], F32))
                fsg = _px.enter_context(nc.sbuf_tensor("pp_fsg", [128, 8 * 1024], BF16))
                chat2 = _px.enter_context(nc.sbuf_tensor("pp_ch2", [128, 32 * 128], BF16))
                y1b = _px.enter_context(nc.sbuf_tensor("pp_y1", [128, DI], BF16))
                zsb = _px.enter_context(nc.sbuf_tensor("pp_zs", [128, DI], BF16))
                yb = _px.enter_context(nc.sbuf_tensor("pp_yb", [128, DI], BF16))
                yfm = _px.enter_context(nc.sbuf_tensor("pp_yfm", [128, 16 * 512], BF16))
                xnb = _px.enter_context(nc.sbuf_tensor("pp_xn", [128, D], F32))
                s0b = _px.enter_context(nc.sbuf_tensor("pp_s0", [128, D], F32))
                r1b = _px.enter_context(nc.sbuf_tensor("pp_r1", [128, D], F32))
                sxb = _px.enter_context(nc.sbuf_tensor("pp_sx", [128, D], F32))
                sf1 = _px.enter_context(nc.sbuf_tensor("pp_sf", [128, 512], F32))
                h2b = _px.enter_context(nc.sbuf_tensor("pp_h2", [128, D], BF16))
                n2w = _px.enter_context(nc.sbuf_tensor("pp_n2w", [128, D], F32))
                dma(SY, dS, n2w[:], n2w_c)
                dma(GP, dG, woutr.ap().rearrange("p (c n) -> p c n", c=16),
                    sp_wout.rearrange("(c p) n -> p c n", p=128))
                for ti in range(NT):
                    t0 = ti * 512
                    SY.wait_ge(cV.h, cV.n)
                    dma(SY, dS, fsl[:], sp_fs[ti])
                    dS.wait(VE)
                    for cit in range(8):
                        c = ti * 8 + cit
                        for g in range(16):
                            VE.scalar_tensor_tensor(
                                fsg[:, cit * 1024 + g * 64:cit * 1024 + (g + 1) * 64],
                                carry_rm[:, g * 64:(g + 1) * 64],
                                pc_bc[:, g * NCH + c:g * NCH + c + 1],
                                fsl[:, cit * 1024 + g * 64:cit * 1024 + (g + 1) * 64],
                                ALU.mult, ALU.add)
                    for tb4 in range(TBT):
                        tb = ti * TBT + tb4
                        sl = slice(tb * 128, (tb + 1) * 128)
                        SY.wait_ge(cP.h, cP.n)
                        SY.wait_ge(cV.h, cV.n)
                        for nch in range(32):
                            dma(SY, dS, chat2[:, nch * 128:(nch + 1) * 128],
                                sp_c2[sl, nch * 128:(nch + 1) * 128], transpose=True)
                        dma(SY, dS, y1b[:], sp_y1[sl, :])
                        dma(SY, dS, zsb[:], sp_zs[sl, :])
                        dma(SY, dS, xnb[:], sp_xn[sl, :])
                        dma(SY, dS, s0b[:], sp_s0[sl, :])
                        dS.wait(PE)
                        PE.wait_ge(cV.h, cV.n)
                        for hg in range(4):
                            bank = pb[2 + hg % 2]
                            PE.wait_ge(cV.h, cV.n)
                            for hh in range(8):
                                h = hg * 8 + hh
                                g = h // 2
                                for c2 in range(2):
                                    cit = tb4 * 2 + c2
                                    PE.matmul(
                                        bank[64 * c2:64 * c2 + 64, hh * 64:(hh + 1) * 64],
                                        chat2[:, h * 128 + c2 * 64:h * 128 + c2 * 64 + 64],
                                        fsg[:, cit * 1024 + g * 64:cit * 1024 + (g + 1) * 64],
                                        start=True, stop=True)
                            cP.inc(PE.last)
                            VE.wait_ge(cP.h, cP.n)
                            VE.tensor_tensor(sf1[:, :], bank[:, :],
                                             y1b[:, hg * 512:(hg + 1) * 512], ALU.add)
                            VE.tensor_tensor(yb[:, hg * 512:(hg + 1) * 512], sf1[:, :],
                                             zsb[:, hg * 512:(hg + 1) * 512], ALU.mult)
                        SY.wait_ge(cV.h, cV.n)
                        dma(SY, dS, sp_y[sl, :], yb[:])
                    dS.wait(SY)
                    for dc in range(16):
                        dma(SY, dS, yfm[:, dc * 512:(dc + 1) * 512],
                            sp_y[t0:t0 + 512, dc * 128:(dc + 1) * 128], transpose=True)
                    for tb4 in range(TBT):
                        tb = ti * TBT + tb4
                        sl = slice(tb * 128, (tb + 1) * 128)
                        dS.wait(PE)
                        dG.wait(PE)
                        PE.wait_ge(cA.h, cA.n)
                        PE.wait_ge(cV.h, cV.n)
                        for dc in range(16):
                            PE.matmul(pb[4][:, :],
                                      yfm[:, dc * 512 + tb4 * 128:dc * 512 + (tb4 + 1) * 128],
                                      woutr[:, dc * D:dc * D + 512],
                                      start=(dc == 0), stop=(dc == 15))
                            PE.matmul(pb[5][:, :],
                                      yfm[:, dc * 512 + tb4 * 128:dc * 512 + (tb4 + 1) * 128],
                                      woutr[:, dc * D + 512:(dc + 1) * D],
                                      start=(dc == 0), stop=(dc == 15))
                        cP.inc(PE.last)
                        VE.wait_ge(cP.h, cP.n)
                        for half in range(2):
                            bank = pb[4 + half]
                            hsl = slice(half * 512, (half + 1) * 512)
                            VE.tensor_tensor(r1b[:, hsl], bank[:, :], xnb[:, hsl], ALU.add)
                            VE.tensor_tensor(sxb[:, hsl], bank[:, :], s0b[:, hsl], ALU.add)
                        SY.wait_ge(cV.h, cV.n)
                        dma(SY, dS, sp_sx[sl, :], sxb[:])
                        SC.wait_ge(cV.h, cV.n)
                        SC.activation(sf1[:, :], r1b[:, 0:512], AF.Square,
                                      accum_out=sm[:, 0:1])
                        SC.activation(sf1[:, :], r1b[:, 512:1024], AF.Square,
                                      accum_out=sm[:, 1:2])
                        VE.wait_ge(cA.h, cA.n)
                        VE.tensor_add(sm[:, 2:3], sm[:, 0:1], sm[:, 1:2])
                        VE.tensor_scalar(sm[:, 3:4], sm[:, 2:3], 1.0 / D, 1e-5,
                                         ALU.mult, ALU.add)
                        SC.wait_ge(cV.h, cV.n)
                        SC.activation(sm[:, 4:5], sm[:, 3:4], AF.Sqrt)
                        VE.wait_ge(cA.h, cA.n)
                        VE.reciprocal(sm[:, 5:6], sm[:, 4:5])
                        VE.scalar_tensor_tensor(h2b[:], r1b[:], sm[:, 5:6], n2w[:],
                                                ALU.mult, ALU.mult)
                        SY.wait_ge(cV.h, cV.n)
                        dma(SY, dS, sp_h2[sl, :], h2b[:])
                        dS.wait(SY)
                        dS.wait(VE)
                fence()

        if '2a' in phases:
            with contextlib.ExitStack() as _px:
                wgr = _px.enter_context(nc.sbuf_tensor("pa_wg", [128, 8 * DMLP], BF16))
                wur = _px.enter_context(nc.sbuf_tensor("pa_wu", [128, 8 * DMLP], BF16))
                wdr = _px.enter_context(nc.sbuf_tensor("pa_wd", [128, 20 * 1024], BF16))
                h2f = _px.enter_context(nc.sbuf_tensor("pa_h2f", [128, 8 * 128], BF16))
                gb = _px.enter_context(nc.sbuf_tensor("pa_g", [128, DMLP], BF16))
                gt = _px.enter_context(nc.sbuf_tensor("pa_gt", [128, DMLP], BF16))
                gf = _px.enter_context(nc.sbuf_tensor("pa_gf", [128, 20 * 128], BF16))
                sxb = _px.enter_context(nc.sbuf_tensor("pa_sx", [128, D], F32))
                acb = _px.enter_context(nc.sbuf_tensor("pa_ac", [128, D], F32))
                dma(GP, dG, wgr.ap().rearrange("p (c n) -> p c n", c=8),
                    sp_wg.rearrange("(c p) n -> p c n", p=128))
                dma(GP, dG, wur.ap().rearrange("p (c n) -> p c n", c=8),
                    sp_wu.rearrange("(c p) n -> p c n", p=128))
                dma(GP, dG, wdr.ap().rearrange("p (c n) -> p c n", c=20),
                    sp_wd.rearrange("(c p) n -> p c n", p=128))
                for tb in range(NB):
                    sl = slice(tb * 128, (tb + 1) * 128)
                    SY.wait_ge(cP.h, cP.n)
                    for dc in range(8):
                        dma(SY, dS, h2f[:, dc * 128:(dc + 1) * 128],
                            sp_h2[sl, dc * 128:(dc + 1) * 128], transpose=True)
                    dS.wait(PE)
                    dG.wait(PE)
                    for s5 in range(5):
                        bank = pb[s5 % 2]
                        PE.wait_ge(cA.h, cA.n)
                        for dc in range(8):
                            PE.matmul(bank[:, :], h2f[:, dc * 128:(dc + 1) * 128],
                                      wgr[:, dc * DMLP + s5 * 512:dc * DMLP + (s5 + 1) * 512],
                                      start=(dc == 0), stop=(dc == 7))
                        cP.inc(PE.last)
                        SC.wait_ge(cP.h, cP.n)
                        SC.activation(gb[:, s5 * 512:(s5 + 1) * 512], bank[:, :], AF.Silu)
                    for s5 in range(5):
                        bank = pb[2 + s5 % 2]
                        PE.wait_ge(cV.h, cV.n)
                        for dc in range(8):
                            PE.matmul(bank[:, :], h2f[:, dc * 128:(dc + 1) * 128],
                                      wur[:, dc * DMLP + s5 * 512:dc * DMLP + (s5 + 1) * 512],
                                      start=(dc == 0), stop=(dc == 7))
                        cP.inc(PE.last)
                        VE.wait_ge(cP.h, cP.n)
                        VE.wait_ge(cA.h, cA.n)
                        VE.tensor_tensor(gt[:, s5 * 512:(s5 + 1) * 512], bank[:, :],
                                         gb[:, s5 * 512:(s5 + 1) * 512], ALU.mult)
                    SY.wait_ge(cV.h, cV.n)
                    dma(SY, dS, sp_gated[sl, :], gt[:])
                    dS.wait(SY)
                    for kc in range(20):
                        dma(SY, dS, gf[:, kc * 128:(kc + 1) * 128],
                            sp_gated[sl, kc * 128:(kc + 1) * 128], transpose=True)
                    dma(SY, dS, sxb[:], sp_sx[sl, :])
                    dS.wait(PE)
                    PE.wait_ge(cV.h, cV.n)
                    for half in range(2):
                        bank = pb[4 + half]
                        for kc in range(20):
                            PE.matmul(bank[:, :], gf[:, kc * 128:(kc + 1) * 128],
                                      wdr[:, kc * 1024 + half * 512:kc * 1024 + (half + 1) * 512],
                                      start=(kc == 0), stop=(kc == 19))
                    cP.inc(PE.last)
                    VE.wait_ge(cP.h, cP.n)
                    VE.tensor_tensor(acb[:, 0:512], pb[4][:, :], sxb[:, 0:512], ALU.add)
                    VE.tensor_tensor(acb[:, 512:1024], pb[5][:, :], sxb[:, 512:1024], ALU.add)
                    SY.wait_ge(cV.h, cV.n)
                    dma(SY, dS, sp_acc[sl, :], acb[:])
                    dS.wait(SY)
                    dS.wait(VE)
                fence()

        if '2b' in phases:
            with contextlib.ExitStack() as _px:
                xnf = _px.enter_context(nc.sbuf_tensor("pf_xnf", [128, 8 * 512], BF16))
                wqa = _px.enter_context(nc.sbuf_tensor("pf_wqa", [128, 8 * 512], BF16))
                wqb = _px.enter_context(nc.sbuf_tensor("pf_wqb", [128, 8 * 512], BF16))
                hft = _px.enter_context(nc.sbuf_tensor("pf_hf", [128, TBT * HFF], BF16))
                mus = _px.enter_context(nc.sbuf_tensor("pf_mu", [128, TBT * 8], F32))
                m2s = _px.enter_context(nc.sbuf_tensor("pf_m2", [128, TBT * 8], F32))
                sf1 = _px.enter_context(nc.sbuf_tensor("pf_sf", [128, 512], F32))
                hnb = _px.enter_context(nc.sbuf_tensor("pf_hn", [128, HFF], BF16))
                hffnfm = _px.enter_context(nc.sbuf_tensor("pf_hfm", [128, 32 * 512], BF16))
                wp1 = _px.enter_context(nc.sbuf_tensor("pf_wp1", [128, 8 * 1024], BF16))
                wp2 = _px.enter_context(nc.sbuf_tensor("pf_wp2", [128, 8 * 1024], BF16))
                acb = _px.enter_context(nc.sbuf_tensor("pf_ac", [128, D], F32))
                outb = _px.enter_context(nc.sbuf_tensor("pf_out", [128, D], F32))
                for ti in range(NT):
                    t0 = ti * 512
                    SY.wait_ge(cP.h, cP.n)
                    for dc in range(8):
                        dma(SY, dS, xnf[:, dc * 512:(dc + 1) * 512],
                            sp_xn16[t0:t0 + 512, dc * 128:(dc + 1) * 128], transpose=True)
                    for cs in range(8):
                        wbuf = wqa if cs % 2 == 0 else wqb
                        SY.wait_ge(cP.h, cP.n)
                        dma(SY, dS, wbuf.ap().rearrange("p (c n) -> p c n", c=8),
                            sp_wqf.rearrange("(c p) n -> p c n", p=128)
                            [:, :, cs * 512:(cs + 1) * 512])
                        for tb4 in range(TBT):
                            bank = pb[(cs * TBT + tb4) % 2]
                            dS.wait(PE)
                            PE.wait_ge(cA.h, cA.n)
                            for dc in range(8):
                                PE.matmul(bank[:, :],
                                          xnf[:, dc * 512 + tb4 * 128:dc * 512 + (tb4 + 1) * 128],
                                          wbuf[:, dc * 512:(dc + 1) * 512],
                                          start=(dc == 0), stop=(dc == 7))
                            cP.inc(PE.last)
                            SC.wait_ge(cP.h, cP.n)
                            SC.activation(hft[:, tb4 * HFF + cs * 512:tb4 * HFF + (cs + 1) * 512],
                                          bank[:, :], AF.Gelu, scale=0.5,
                                          accum_out=mus[:, tb4 * 8 + cs:tb4 * 8 + cs + 1])
                            SC.wait_ge(cA.h, cA.n)
                            SC.activation(sf1[:, :],
                                          hft[:, tb4 * HFF + cs * 512:tb4 * HFF + (cs + 1) * 512],
                                          AF.Square,
                                          accum_out=m2s[:, tb4 * 8 + cs:tb4 * 8 + cs + 1])
                    for tb4 in range(TBT):
                        tb = ti * TBT + tb4
                        sl = slice(tb * 128, (tb + 1) * 128)
                        VE.wait_ge(cA.h, cA.n)
                        dS.wait(VE)
                        VE.tensor_reduce(sm[:, 0:1], mus[:, tb4 * 8:(tb4 + 1) * 8], AX.X, ALU.add)
                        VE.tensor_reduce(sm[:, 1:2], m2s[:, tb4 * 8:(tb4 + 1) * 8], AX.X, ALU.add)
                        VE.tensor_scalar(sm[:, 2:3], sm[:, 0:1], 1.0 / HFF, None, ALU.mult)
                        VE.tensor_mul(sm[:, 3:4], sm[:, 2:3], sm[:, 2:3])
                        VE.scalar_tensor_tensor(sm[:, 4:5], sm[:, 1:2], 1.0 / HFF, sm[:, 3:4],
                                                ALU.mult, ALU.subtract)
                        SC.wait_ge(cV.h, cV.n)
                        SC.activation(sm[:, 5:6], sm[:, 4:5], AF.Sqrt, bias=epsln[:])
                        VE.wait_ge(cA.h, cA.n)
                        VE.reciprocal(sm[:, 6:7], sm[:, 5:6])
                        VE.tensor_scalar(hnb[:], hft[:, tb4 * HFF:(tb4 + 1) * HFF],
                                         sm[:, 2:3], sm[:, 6:7], ALU.subtract, ALU.mult)
                        SY.wait_ge(cV.h, cV.n)
                        dma(SY, dS, sp_hffn[sl, :], hnb[:])
                    dS.wait(SY)
                    for kc in range(32):
                        dma(SY, dS, hffnfm[:, kc * 512:(kc + 1) * 512],
                            sp_hffn[t0:t0 + 512, kc * 128:(kc + 1) * 128], transpose=True)
                    for kcg in range(4):
                        wbuf = wp1 if kcg % 2 == 0 else wp2
                        SY.wait_ge(cP.h, cP.n)
                        dma(SY, dS, wbuf.ap().rearrange("p (c n) -> p c n", c=8),
                            sp_wqp.rearrange("(c p) n -> p c n", p=128)
                            [:, kcg * 8:(kcg + 1) * 8, :])
                        dS.wait(PE)
                        if kcg == 0:
                            PE.wait_ge(cV.h, cV.n)
                            PE.wait_ge(cA.h, cA.n)
                        for tb4 in range(TBT):
                            for k8 in range(8):
                                kc = kcg * 8 + k8
                                for half in range(2):
                                    PE.matmul(
                                        pb[tb4 * 2 + half][:, :],
                                        hffnfm[:, kc * 512 + tb4 * 128:kc * 512 + (tb4 + 1) * 128],
                                        wbuf[:, k8 * 1024 + half * 512:k8 * 1024 + (half + 1) * 512],
                                        start=(kc == 0), stop=False)
                        cP.inc(PE.last)
                    for tb4 in range(TBT):
                        tb = ti * TBT + tb4
                        sl = slice(tb * 128, (tb + 1) * 128)
                        for half in range(2):
                            PE.matmul(pb[tb4 * 2 + half][:, :], onesrow[:],
                                      lnbw_row[0:1, half * 512:(half + 1) * 512],
                                      start=False, stop=True)
                        cP.inc(PE.last)
                        SY.wait_ge(cV.h, cV.n)
                        dma(SY, dS, acb[:], sp_acc[sl, :])
                        VE.wait_ge(cP.h, cP.n)
                        dS.wait(VE)
                        for half in range(2):
                            hsl = slice(half * 512, (half + 1) * 512)
                            VE.scalar_tensor_tensor(outb[:, hsl], pb[tb4 * 2 + half][:, :],
                                                    0.5, acb[:, hsl], ALU.mult, ALU.add)
                        SY.wait_ge(cV.h, cV.n)
                        dma(SY, dS, out_hbm[sl, :], outb[:])
                        dS.wait(SY)
                        dS.wait(VE)
                fence()

    return nc


_PERM = None


def _nperm():
    p = np.arange(DS).reshape(DS // 2, 2).T.reshape(-1)
    return p


def _prep_inputs(inputs):
    f32 = np.float32
    import ml_dtypes
    bf16 = ml_dtypes.bfloat16
    W_in = np.asarray(inputs["L0_attn_W_in"], f32)
    perm = _nperm()
    cols = np.arange(MAINC)
    main_idx = np.concatenate([
        np.arange(0, 4096),
        np.concatenate([4096 + h * DS + perm for h in range(NH)]),
        np.concatenate([6144 + h * DS + perm for h in range(NH)]),
    ])
    w_main = np.ascontiguousarray(W_in[:, main_idx])
    w_aux = np.ascontiguousarray(W_in[:, 8192:])

    def rep(v, n=128):
        return np.ascontiguousarray(np.broadcast_to(np.asarray(v, f32)[None, :], (n, len(v))))

    n1w = rep(inputs["L0_attn_norm1_w"])
    n2w = rep(inputs["L0_attn_norm2_w"])
    bnw = rep(np.asarray(inputs["L0_attn_Bn_w"], f32)[perm]).astype(bf16)
    cnw = rep(np.asarray(inputs["L0_attn_Cn_w"], f32)[perm]).astype(bf16)
    bb = np.asarray(inputs["L0_attn_B_bias"], f32).reshape(NH, DS)[:, perm].reshape(-1)
    cb = np.asarray(inputs["L0_attn_C_bias"], f32).reshape(NH, DS)[:, perm].reshape(-1)
    bbias = rep(bb).astype(bf16)
    cbias = rep(cb).astype(bf16)
    dparam = rep(inputs["L0_attn_D_param"])
    dtb = np.asarray(inputs["L0_attn_dt_bias"], f32)
    alog = np.asarray(inputs["L0_attn_A_log"], f32)
    dtbias_tm = rep(dtb)
    dtbias_pk = np.ascontiguousarray(np.tile(dtb, 4)[:, None])
    alog_pk = np.ascontiguousarray(np.tile(alog, 4)[:, None])
    lnw = np.ascontiguousarray(np.asarray(inputs["L0_ffn_ln_w"], f32).reshape(32, 128).T)
    lnb16 = np.ascontiguousarray(
        np.asarray(inputs["L0_ffn_ln_b"], f32).reshape(32, 128).T).astype(bf16)
    tri = (np.tril(np.ones((128, 128), f32)).T * INV2PI).astype(f32)
    tri = np.ascontiguousarray(tri)
    tt = np.arange(512)
    mask01 = (tt % 64 != 0).astype(f32)
    c63 = (63 - (tt % 64)).astype(f32)
    fmc = np.zeros((128, 3 * 512), f32)
    fmc[:, 0:512] = mask01[None, :]
    fmc[:, 512:1024] = c63[None, :]
    k = np.arange(128)
    m = np.arange(128)
    shiftsum = (((k[:, None] % 32) == (m[None, :] % 32)) &
                ((k[:, None] // 32) < (m[None, :] // 32))).astype(f32)
    sel = np.zeros((128, 16 * 128), f32)
    for g in range(16):
        for mm in range(128):
            kk = 2 * g + mm // 64
            sel[np.arange(4) * 32 + kk, g * 128 + mm] = 1.0
    ident = np.eye(128, dtype=f32)

    com = dict(
        w_main=w_main, w_aux=w_aux,
        w_out=np.asarray(inputs["L0_attn_W_out"], f32),
        w_g=np.asarray(inputs["L0_attn_Wg"], f32),
        w_u=np.asarray(inputs["L0_attn_Wu"], f32),
        w_d=np.asarray(inputs["L0_attn_Wd"], f32),
        w_fct=np.ascontiguousarray(np.asarray(inputs["L0_ffn_W_fc"], f32).T),
        w_projt=np.ascontiguousarray(np.asarray(inputs["L0_ffn_W_proj"], f32).T),
        n1w=n1w, n2w=n2w, bnw=bnw, cnw=cnw, bbias=bbias, cbias=cbias,
        dparam=dparam, dtbias_tm=dtbias_tm, dtbias_pk=dtbias_pk,
        nalog_pk=alog_pk, alog_row=np.ascontiguousarray(alog[None, :]),
        fmc=fmc, shiftsum=shiftsum, lnw_c=lnw, lnb16_c=lnb16,
        tri_c=tri, sel_c=sel, identf_c=ident,
    )
    x = np.asarray(inputs["L0_attn_x"], f32)
    in_maps = []
    for core in range(8):
        b, half = core // 2, core % 2
        shard = np.ascontiguousarray(x[b, half * T:(half + 1) * T, :])
        mask = np.full((128, 1), float(half), f32)
        m = dict(com)
        m["x"] = shard
        m["mask_c"] = mask
        in_maps.append(m)
    return in_maps


_CACHED = {}


def kernel(**inputs):
    key = "nc"
    if key not in _CACHED:
        _CACHED[key] = build()
    nc = _CACHED[key]
    in_maps = _prep_inputs(inputs)
    res = run_bass_kernel_spmd(nc, in_maps, core_ids=list(range(8)))
    out = np.zeros((4, 4096, 1024), np.float32)
    for core in range(8):
        b, half = core // 2, core % 2
        out[b, half * T:(half + 1) * T, :] = res.results[core]["out"]
    return out


# version 04 (final)
# speedup vs baseline: 1.1319x; 1.1319x over previous
import contextlib
import dataclasses

import numpy as np

import concourse.bass as bass
import concourse.mybir as mybir
from concourse.bass_utils import run_bass_kernel_spmd

F32 = mybir.dt.float32
BF16 = mybir.dt.bfloat16
I32 = mybir.dt.int32
AF = mybir.ActivationFunctionType
ALU = mybir.AluOpType
AX = mybir.AxisListType

PI = float(np.pi)
INV2PI = 1.0 / (2.0 * PI)

D = 1024
T = 2048
NB = 16
NT = 4
TBT = 4
NH = 32
DS = 64
HD = 64
NCH = 32
DI = 2048
DMLP = 2560
HFF = 4096
AUXC = 1056
MAINC = 8192
ZO, XO, BO, CO = 0, 2048, 4096, 6144


def _bc(ap2d, groups, off=0):
    return dataclasses.replace(
        ap2d, offset=ap2d.offset + off,
        ap=[list(ap2d.ap[0])] + [list(g) for g in groups])



class _EngW:
    __slots__ = ("_e", "last", "sem")

    def __init__(self, eng, sem=None):
        self._e = eng
        self.last = None
        self.sem = sem

    def __getattr__(self, k):
        f = getattr(self._e, k)
        if callable(f) and k not in ("wait_ge", "nop", "print", "dma_start",
                                     "collective_compute"):
            def g(*a, **kw):
                if self.sem is not None and self.sem.n > 0:
                    self._e.wait_ge(self.sem.h, self.sem.n)
                r = f(*a, **kw)
                self.last = r
                if self.sem is not None:
                    self.sem.n += 1
                    r.then_inc(self.sem.h, 1)
                return r
            return g
        return f

class Sem:
    def __init__(self, h):
        self.h = h
        self.n = 0
        self.maxwait = 0
        self.lvl = {}

    def inc(self, instr, k=1):
        self.n += k
        instr.then_inc(self.h, k)

    def wait(self, eng):
        if self.n > 0:
            e = eng._e if isinstance(eng, _EngW) else eng
            e.wait_ge(self.h, self.n)
            self.maxwait = max(self.maxwait, self.n)
            self.lvl[id(e)] = max(self.lvl.get(id(e), 0), self.n)


def build(debug=(), phases=('wprep', 'p0', 'p1', 'post', '2a', '2b'), p1cut=5, p1grp=3):
    nc = bass.Bass("TRN2", target_bir_lowering=False)

    def din(name, shape, dtype=F32):
        return nc.dram_tensor(name, shape, dtype, kind="ExternalInput").ap()

    def dsc(name, shape, dtype):
        kind = "ExternalOutput" if name in debug else "Internal"
        return nc.dram_tensor(name, shape, dtype, kind=kind).ap()

    x_in = din("x", [T, D])
    w_main = din("w_main", [D, MAINC])
    w_aux = din("w_aux", [D, AUXC])
    w_out = din("w_out", [DI, D])
    w_g = din("w_g", [D, DMLP])
    w_u = din("w_u", [D, DMLP])
    w_d = din("w_d", [DMLP, D])
    w_fct = din("w_fct", [D, HFF])
    w_projt = din("w_projt", [HFF, D])
    n1w_c = din("n1w", [128, D])
    n2w_c = din("n2w", [128, D])
    bnw_c = din("bnw", [128, DS], BF16)
    cnw_c = din("cnw", [128, DS], BF16)
    bbias_c = din("bbias", [128, DI], BF16)
    cbias_c = din("cbias", [128, DI], BF16)
    dparam_c = din("dparam", [128, NH])
    dtbias_tm_c = din("dtbias_tm", [128, NH])
    dtbias_pk_c = din("dtbias_pk", [128, 1])
    nalog_pk_c = din("nalog_pk", [128, 1])
    alog_row_c = din("alog_row", [1, NH])
    fmc_c = din("fmc", [128, 3 * 512])
    shiftsum_c = din("shiftsum", [128, 128])
    lnw_c = din("lnw_c", [128, 32])
    lnb_c = din("lnb16_c", [128, 32], BF16)
    tri_c = din("tri_c", [128, 128])
    sel_c = din("sel_c", [128, 16 * 128])
    mask_c = din("mask_c", [128, 1])
    ident_c = din("identf_c", [128, 128])

    out_hbm = nc.dram_tensor("out", [T, D], F32, kind="ExternalOutput").ap()

    sp_win = dsc("sp_win", [D, MAINC], BF16)
    sp_wout = dsc("sp_wout", [DI, D], BF16)
    sp_wg = dsc("sp_wg", [D, DMLP], BF16)
    sp_wu = dsc("sp_wu", [D, DMLP], BF16)
    sp_wd = dsc("sp_wd", [DMLP, D], BF16)
    sp_wqf = dsc("sp_wqf", [D, HFF], BF16)
    sp_wqp = dsc("sp_wqp", [HFF, D], BF16)
    sp_h1 = dsc("sp_h1", [T, D], BF16)
    sp_h1lo = dsc("sp_h1lo", [T, D], BF16)
    sp_fm3 = dsc("sp_fm3", [128, 3 * 512], BF16)
    sp_dec = dsc("sp_dec", [128, 8], F32)
    sp_pc = dsc("sp_pc", [128, 8], F32)
    sp_xn16 = dsc("sp_xn16", [T, D], BF16)
    sp_xn = dsc("sp_xn", [T, D], F32)
    sp_s0 = dsc("sp_s0", [T, D], F32)
    sp_p0 = dsc("sp_p0", [T, D], F32)
    sp_cs = dsc("sp_cs", [T, 2 * D], BF16)
    sp_zs = dsc("sp_zs", [T, DI], BF16)
    sp_br = dsc("sp_br", [T, 2 * DI], BF16)
    sp_c1 = dsc("sp_c1", [T, 2 * DI], BF16)
    sp_c2 = dsc("sp_c2", [T, 2 * DI], BF16)
    sp_y1 = dsc("sp_y1", [T, DI], BF16)
    sp_fs = dsc("sp_fs", [NT, 128, 8 * 1024], F32)
    sp_y = dsc("sp_y", [T, DI], BF16)
    sp_sx = dsc("sp_sx", [T, D], F32)
    sp_h2 = dsc("sp_h2", [T, D], BF16)
    sp_gated = dsc("sp_gated", [T, DMLP], BF16)
    sp_acc = dsc("sp_acc", [T, D], F32)
    sp_hffn = dsc("sp_hffn", [T, HFF], BF16)
    cc1i = nc.dram_tensor("cc1i", [1, D], F32).ap()
    cc1o = nc.dram_tensor("cc1o", [2, D], F32).ap()
    cc2i = nc.dram_tensor("cc2i", [128, 1024], F32).ap()
    cc2o = nc.dram_tensor("cc2o", [2, 128, 1024], F32).ap()

    ctx = contextlib.ExitStack()
    with ctx:
        E = ctx.enter_context
        dts_tm = E(nc.sbuf_tensor("dts_tm", [128, NB * NH], F32))
        gam_tm = E(nc.sbuf_tensor("gam_tm", [128, NB * NH], F32))
        lsum_tm = E(nc.sbuf_tensor("lsum_tm", [128, NB * NH], F32))
        expa_tm = E(nc.sbuf_tensor("expa_tm", [128, NB * NH], F32))
        ds_tm = E(nc.sbuf_tensor("ds_tm", [128, NB * NH], F32))
        dec_bc = E(nc.sbuf_tensor("dec_bc", [128, 16 * NCH], F32))
        pc_bc = E(nc.sbuf_tensor("pc_bc", [128, 16 * NCH], F32))
        offrow = E(nc.sbuf_tensor("offrow", [1, D], F32))
        ccrecv = E(nc.sbuf_tensor("ccrecv", [1, D], F32))
        carry_rm = E(nc.sbuf_tensor("carry_rm", [128, 1024], F32))
        fslast = E(nc.sbuf_tensor("fslast", [128, 1024], F32))
        lnbw_row = E(nc.sbuf_tensor("lnbw_row", [1, D], F32))
        maskc = E(nc.sbuf_tensor("maskc", [128, 1], F32))
        identf = E(nc.sbuf_tensor("identf", [128, 128], F32))
        identb = E(nc.sbuf_tensor("identb", [128, 128], BF16))
        onescol = E(nc.sbuf_tensor("onescol", [128, 1], F32))
        ones32c = E(nc.sbuf_tensor("ones32c", [128, 32], F32))
        turncol = E(nc.sbuf_tensor("turncol", [128, 1], F32))
        turn32 = E(nc.sbuf_tensor("turn32", [128, 32], F32))
        onesrow = E(nc.sbuf_tensor("onesrow", [1, 128], F32))
        onecol = E(nc.sbuf_tensor("onecol", [128, 1], F32))
        halfpi = E(nc.sbuf_tensor("halfpi", [128, 1], F32))
        epsln = E(nc.sbuf_tensor("epsln", [128, 1], F32))
        thrq = E(nc.sbuf_tensor("thrq", [128, 1], F32))
        nthrq = E(nc.sbuf_tensor("nthrq", [128, 1], F32))
        sm = E(nc.sbuf_tensor("sm", [128, 16], F32))
        sm2 = E(nc.sbuf_tensor("sm2", [128, NH], F32))
        sm3 = E(nc.sbuf_tensor("sm3", [128, NH], F32))
        skip_t = E(nc.sbuf_tensor("skip_t", [128, TBT * NH], F32))
        pb = [E(nc.psum_tensor("pb%d" % i, [128, 512], F32)) for i in range(8)]

        dS = Sem(E(nc.semaphore("dS")))
        dG = Sem(E(nc.semaphore("dG")))
        dA = Sem(E(nc.semaphore("dA")))
        cP = Sem(E(nc.semaphore("cP")))
        cV = Sem(E(nc.semaphore("cV")))
        cA = Sem(E(nc.semaphore("cA")))
        ccs = Sem(E(nc.semaphore("ccs")))
        allsems = [dS, dG, dA, cP, cV, cA, ccs]

        blk = E(nc.Block())
        SY = _EngW(nc.sync)
        VE = _EngW(nc.vector, cV)
        SC = _EngW(nc.scalar, cA)
        PE = _EngW(nc.tensor)
        GP = _EngW(nc.gpsimd)

        def fence():
            for s in allsems:
                if s.n:
                    for eng in (SY, VE, SC, PE, GP):
                        eng.wait_ge(s.h, s.n)

        def dma(eng, sem, out, in_, **kw):
            e = eng._e if isinstance(eng, _EngW) else eng
            if sem.maxwait > sem.lvl.get(id(e), 0):
                e.wait_ge(sem.h, sem.maxwait)
                sem.lvl[id(e)] = sem.maxwait
            sem.inc(eng.dma_start(out=out, in_=in_, **kw), 16)

        def vsync():
            VE.wait_ge(cV.h, cV.n)

        tr_snap = {}

        def trwave(pairs, b0=6):
            PE.wait_ge(cV.h, cV.n)
            for w0 in range(0, len(pairs), 4):
                bi = b0 + (w0 // 4) % 2
                if w0 >= 8:
                    PE.wait_ge(cV.h, tr_snap[bi])
                bank = pb[bi].ap().bitcast(BF16)
                grp = pairs[w0:w0 + 4]
                for i, (s, _) in enumerate(grp):
                    PE.transpose(bank[:, i * 128:(i + 1) * 128], s, identb[:])
                cP.inc(PE.last)
                VE.wait_ge(cP.h, cP.n)
                for i, (_, dst) in enumerate(grp):
                    VE.tensor_copy(dst, bank[:, i * 128:(i + 1) * 128])
                tr_snap[bi] = cV.n

        dma(SY, dS, identf[:], ident_c)
        dma(SY, dS, maskc[:], mask_c)
        dS.wait(VE)
        VE.tensor_copy(identb[:], identf[:])
        for _t, _v in [(onescol, 1.0), (ones32c, 1.0), (turncol, INV2PI),
                       (turn32, INV2PI), (onesrow, 1.0),
                       (onecol, 1.0), (halfpi, PI / 2.0), (epsln, 1e-5),
                       (offrow, 0.0), (fslast, 0.0)]:
            VE.memset(_t[:], _v)
        VE.wait_ge(cV.h, cV.n)

        if 'wprep' in phases:
            with contextlib.ExitStack() as _px:
                _px.enter_context(nc.named_scope('wprep'))
                tA = _px.enter_context(nc.sbuf_tensor("wp_a", [128, HFF], F32))
                tB = _px.enter_context(nc.sbuf_tensor("wp_b", [128, HFF], F32))
                tS1 = _px.enter_context(nc.sbuf_tensor("wp_s1", [128, HFF], F32))
                tS2 = _px.enter_context(nc.sbuf_tensor("wp_s2", [128, HFF], F32))
                kQ = _px.enter_context(nc.sbuf_tensor("wp_q", [128, MAINC], BF16))
                kQ2 = _px.enter_context(nc.sbuf_tensor("wp_q2", [128, MAINC], BF16))
                qU = _px.enter_context(nc.sbuf_tensor("wp_qu", [128, 1024], BF16))
                lnw = _px.enter_context(nc.sbuf_tensor("wp_lnw", [128, 32], F32))
                lnb16 = _px.enter_context(nc.sbuf_tensor("wp_lnb", [128, 32], BF16))
                dma(SY, dS, lnw[:], lnw_c)
                dma(SY, dS, lnb16[:], lnb_c)

                for src, dst, nch, wd in [
                        (w_main, sp_win, 8, MAINC), (w_out, sp_wout, 16, D),
                        (w_g, sp_wg, 8, DMLP), (w_u, sp_wu, 8, DMLP),
                        (w_d, sp_wd, 20, D)]:
                    for ch in range(nch):
                        q = kQ if ch % 2 == 0 else kQ2
                        dA.wait(GP)
                        dma(GP, dG, q[:, 0:wd], src[ch * 128:(ch + 1) * 128, :])
                        dG.wait(SC)
                        dma(SC, dA, dst[ch * 128:(ch + 1) * 128, :], q[:, 0:wd])

                for (w_ap, q_dst, nch, wd, fold, do_lnbw) in [
                        (w_fct, sp_wqf, 8, HFF, False, False),
                        (w_projt, sp_wqp, 32, D, True, True)]:
                    dA.wait(VE)
                    VE.memset(sm[:, 0:1], 0.0)
                    VE.wait_ge(cV.h, cV.n)
                    for ch in range(nch):
                        buf = tA if ch % 2 == 0 else tB
                        SY.wait_ge(cV.h, cV.n)
                        dma(SY, dS, buf[:, 0:wd], w_ap[ch * 128:(ch + 1) * 128, :])
                        dS.wait(VE)
                        VE.tensor_reduce(sm[:, 1:2], buf[:, 0:wd], AX.X, ALU.add,
                                         apply_absolute_value=True)
                        VE.tensor_add(sm[:, 0:1], sm[:, 0:1], sm[:, 1:2])
                    PE.wait_ge(cV.h, cV.n)
                    PE.matmul(pb[2][0:1, 0:1], sm[:, 0:1], onescol[:], start=True, stop=True)
                    cP.inc(PE.last)
                    VE.wait_ge(cP.h, cP.n)
                    VE.tensor_scalar(sm[0:1, 2:3], pb[2][0:1, 0:1],
                                     1.0 / (nch * 128 * wd), 1e-6, ALU.mult, ALU.add)
                    VE.tensor_scalar(sm[0:1, 3:4], sm[0:1, 2:3], 0.5, None, ALU.mult)
                    PE.wait_ge(cV.h, cV.n)
                    PE.matmul(pb[2][:, 1:2], onesrow[:], sm[0:1, 3:4], start=True, stop=True)
                    cP.inc(PE.last)
                    VE.wait_ge(cP.h, cP.n)
                    VE.tensor_copy(thrq[:], pb[2][:, 1:2])
                    VE.tensor_scalar(nthrq[:], pb[2][:, 1:2], -1.0, None, ALU.mult)
                    for ch in range(nch):
                        buf = tA if ch % 2 == 0 else tB
                        q = kQ if ch % 2 == 0 else kQ2
                        SY.wait_ge(cA.h, cA.n)
                        dma(SY, dS, buf[:, 0:wd], w_ap[ch * 128:(ch + 1) * 128, :])
                        dS.wait(SC)
                        SC.wait_ge(cV.h, cV.n)
                        dA.wait(SC)
                        SC.activation(tS1[:, 0:wd], buf[:, 0:wd], AF.Sign, bias=nthrq[:])
                        SC.activation(tS2[:, 0:wd], buf[:, 0:wd], AF.Sign, bias=thrq[:])
                        VE.wait_ge(cA.h, cA.n)
                        VE.tensor_add(tS1[:, 0:wd], tS1[:, 0:wd], tS2[:, 0:wd])
                        if do_lnbw:
                            VE.tensor_copy(qU[:, 0:wd], tS1[:, 0:wd])
                        if fold:
                            VE.tensor_scalar(q[:, 0:wd], tS1[:, 0:wd], lnw[:, ch:ch + 1],
                                             None, ALU.mult)
                        else:
                            VE.tensor_copy(q[:, 0:wd], tS1[:, 0:wd])
                        if do_lnbw:
                            PE.wait_ge(cV.h, cV.n)
                            PE.matmul(pb[0][0:1, :], lnb16[:, ch:ch + 1], qU[:, 0:512],
                                      start=(ch == 0), stop=(ch == nch - 1))
                            PE.matmul(pb[1][0:1, :], lnb16[:, ch:ch + 1], qU[:, 512:1024],
                                      start=(ch == 0), stop=(ch == nch - 1))
                            cP.inc(PE.last)
                            SC.wait_ge(cP.h, cP.n)
                        else:
                            SC.wait_ge(cV.h, cV.n)
                        dma(SC, dA, q_dst[ch * 128:(ch + 1) * 128, :], q[:, 0:wd])
                    if do_lnbw:
                        VE.wait_ge(cP.h, cP.n)
                        VE.tensor_copy(lnbw_row[0:1, 0:512], pb[0][0:1, :])
                        VE.tensor_copy(lnbw_row[0:1, 512:1024], pb[1][0:1, :])
                fence()

        with nc.sbuf_tensor("hfm", [128, 8 * T], BF16) as hfm:

            if 'p0' in phases:
                with contextlib.ExitStack() as _px:
                    _px.enter_context(nc.named_scope('p0'))
                    xa = _px.enter_context(nc.sbuf_tensor("p0_xa", [128, D], F32))
                    xb = _px.enter_context(nc.sbuf_tensor("p0_xb", [128, D], F32))
                    na = _px.enter_context(nc.sbuf_tensor("p0_na", [128, D], F32))
                    nb = _px.enter_context(nc.sbuf_tensor("p0_nb", [128, D], F32))
                    s0b = _px.enter_context(nc.sbuf_tensor("p0_s0", [128, D], F32))
                    sqb = _px.enter_context(nc.sbuf_tensor("p0_sq", [128, D], F32))
                    k1 = _px.enter_context(nc.sbuf_tensor("p0_k1", [128, D], BF16))
                    k2 = _px.enter_context(nc.sbuf_tensor("p0_k2", [128, D], BF16))
                    n1w = _px.enter_context(nc.sbuf_tensor("p0_n1w", [128, D], F32))
                    waux = _px.enter_context(nc.sbuf_tensor("p0_waux", [128, 8 * AUXC], BF16))
                    wauxlo = _px.enter_context(nc.sbuf_tensor("p0_wauxlo", [128, 8 * AUXC], BF16))
                    wauxf = _px.enter_context(nc.sbuf_tensor("p0_wauxf", [128, AUXC], F32))
                    hlo = _px.enter_context(nc.sbuf_tensor("p0_hlo", [128, 8 * 512], BF16))
                    h1f = _px.enter_context(nc.sbuf_tensor("p0_h1f", [128, D], F32))
                    k3 = _px.enter_context(nc.sbuf_tensor("p0_k3", [128, D], BF16))
                    lsx = _px.enter_context(nc.sbuf_tensor("p0_lsx", [128, NB * NH], F32))
                    alog_row = _px.enter_context(nc.sbuf_tensor("p0_alogr", [1, NH], F32))
                    pP = _px.enter_context(nc.sbuf_tensor("p0_pp", [128, D], F32))
                    f_dts = _px.enter_context(nc.sbuf_tensor("p0_f1", [128, 512], F32))
                    f_A = _px.enter_context(nc.sbuf_tensor("p0_f2", [128, 512], F32))
                    f_alpha = _px.enter_context(nc.sbuf_tensor("p0_f3", [128, 512], F32))
                    f_acum = _px.enter_context(nc.sbuf_tensor("p0_f4", [128, 512], F32))
                    f_t1 = _px.enter_context(nc.sbuf_tensor("p0_f5", [128, 512], F32))
                    f_t2 = _px.enter_context(nc.sbuf_tensor("p0_f6", [128, 512], F32))
                    f_out = _px.enter_context(nc.sbuf_tensor("p0_f7", [128, 512], F32))
                    fmc = _px.enter_context(nc.sbuf_tensor("p0_fc", [128, 3 * 512], F32))
                    onesf = _px.enter_context(nc.sbuf_tensor("p0_one", [128, 512], F32))
                    atot = _px.enter_context(nc.sbuf_tensor("p0_at", [128, 8], F32))
                    at2 = _px.enter_context(nc.sbuf_tensor("p0_at2", [128, 8], F32))
                    tri = _px.enter_context(nc.sbuf_tensor("p0_tri", [128, 128], F32))
                    shiftsum = _px.enter_context(nc.sbuf_tensor("p0_shs", [128, 128], F32))
                    dtbias_tm = _px.enter_context(nc.sbuf_tensor("p0_dtb", [128, NH], F32))
                    dtbias_pk = _px.enter_context(nc.sbuf_tensor("p0_dtbp", [128, 1], F32))
                    alog_pk = _px.enter_context(nc.sbuf_tensor("p0_alp", [128, 1], F32))
                    sinb = _px.enter_context(nc.sbuf_tensor("p0_sin", [128, D], BF16))
                    cosb = _px.enter_context(nc.sbuf_tensor("p0_cos", [128, D], BF16))
                    dma(SY, dS, n1w[:], n1w_c)
                    dma(SY, dS, tri[:], tri_c)
                    dma(SY, dS, shiftsum[:], shiftsum_c)
                    dma(SY, dS, dtbias_tm[:], dtbias_tm_c)
                    dma(SY, dS, dtbias_pk[:], dtbias_pk_c)
                    dma(SY, dS, alog_pk[:], nalog_pk_c)
                    dma(SY, dS, alog_row[:], alog_row_c)
                    dma(SY, dS, fmc[:], fmc_c)
                    for dc in range(8):
                        SY.wait_ge(cV.h, cV.n)
                        dma(SY, dS, wauxf[:], w_aux[dc * 128:(dc + 1) * 128, :])
                        dS.wait(VE)
                        VE.tensor_copy(waux[:, dc * AUXC:(dc + 1) * AUXC], wauxf[:])
                        VE.tensor_sub(wauxf[:], wauxf[:],
                                      waux[:, dc * AUXC:(dc + 1) * AUXC])
                        VE.tensor_copy(wauxlo[:, dc * AUXC:(dc + 1) * AUXC], wauxf[:])
                        dS.wait(SY)
                    dS.wait(VE)
                    VE.memset(onesf[:], 1.0)
                    VE.wait_ge(cV.h, cV.n)

                    for tb in range(NB):
                        xB = xa if tb % 2 == 0 else xb
                        nB = na if tb % 2 == 0 else nb
                        sl = slice(tb * 128, (tb + 1) * 128)
                        dS.wait(SY)
                        SY.wait_ge(cV.h, cV.n)
                        dma(SY, dS, xB[:, :], x_in[sl, :])
                        dS.wait(SC)
                        SC.activation(sqb[:], xB[:], AF.Square, accum_out=sm[:, 0:1])
                        VE.wait_ge(cA.h, cA.n)
                        VE.tensor_reduce(sm[:, 1:2], xB[:], AX.X, ALU.add)
                        VE.tensor_scalar(sm[:, 2:3], sm[:, 1:2], 1.0 / D, None, ALU.mult)
                        VE.tensor_mul(sm[:, 3:4], sm[:, 2:3], sm[:, 2:3])
                        VE.scalar_tensor_tensor(sm[:, 4:5], sm[:, 0:1], 1.0 / D, sm[:, 3:4],
                                                ALU.mult, ALU.subtract)
                        SC.wait_ge(cV.h, cV.n)
                        SC.activation(sm[:, 5:6], sm[:, 4:5], AF.Sqrt, bias=epsln[:])
                        VE.wait_ge(cA.h, cA.n)
                        VE.reciprocal(sm[:, 6:7], sm[:, 5:6])
                        VE.tensor_scalar(nB[:], xB[:], sm[:, 2:3], sm[:, 6:7],
                                         ALU.subtract, ALU.mult)
                        VE.tensor_add(s0b[:], xB[:], nB[:])
                        SC.wait_ge(cV.h, cV.n)
                        SC.activation(k1[:], nB[:], AF.Copy)
                        SC.activation(sqb[:], nB[:], AF.Square, accum_out=sm[:, 7:8])
                        VE.wait_ge(cA.h, cA.n)
                        VE.tensor_scalar(sm[:, 8:9], sm[:, 7:8], 1.0 / D, 1e-5,
                                         ALU.mult, ALU.add)
                        SC.wait_ge(cV.h, cV.n)
                        SC.activation(sm[:, 9:10], sm[:, 8:9], AF.Sqrt)
                        VE.wait_ge(cA.h, cA.n)
                        VE.reciprocal(sm[:, 10:11], sm[:, 9:10])
                        VE.scalar_tensor_tensor(h1f[:], nB[:], sm[:, 10:11], n1w[:],
                                                ALU.mult, ALU.mult)
                        VE.tensor_copy(k2[:], h1f[:])
                        VE.tensor_sub(h1f[:], h1f[:], k2[:])
                        VE.tensor_copy(k3[:], h1f[:])
                        SY.wait_ge(cV.h, cV.n)
                        dma(SY, dS, sp_xn[sl, :], nB[:])
                        dma(SY, dS, sp_s0[sl, :], s0b[:])
                        dma(SY, dS, sp_xn16[sl, :], k1[:])
                        dma(SY, dS, sp_h1[sl, :], k2[:])
                        dma(SY, dS, sp_h1lo[sl, :], k3[:])

                    dS.wait(SY)
                    for dc in range(8):
                        dma(SY, dS, hfm[:, dc * T:(dc + 1) * T],
                            sp_h1[:, dc * 128:(dc + 1) * 128], transpose=True)

                    dS.wait(PE)
                    dG.wait(PE)
                    for q in range(4):
                        SY.wait_ge(cP.h, cP.n)
                        for dc in range(8):
                            dma(SY, dS, hlo[:, dc * 512:(dc + 1) * 512],
                                sp_h1lo[q * 512:(q + 1) * 512, dc * 128:(dc + 1) * 128],
                                transpose=True)
                        dS.wait(PE)
                        for dc in range(8):
                            for wop, hop, fst, lst in (
                                    (waux, hfm[:, dc * T + q * 512:dc * T + (q + 1) * 512],
                                     dc == 0, False),
                                    (wauxlo, hfm[:, dc * T + q * 512:dc * T + (q + 1) * 512],
                                     False, False),
                                    (waux, hlo[:, dc * 512:(dc + 1) * 512],
                                     False, dc == 7)):
                                PE.matmul(pb[3][q * 32:(q + 1) * 32, :],
                                          wop[:, dc * AUXC:dc * AUXC + NH],
                                          hop, start=fst, stop=lst,
                                          tile_position=(0, q * 32))
                        cP.inc(PE.last)
                    VE.wait_ge(cP.h, cP.n)
                    VE.tensor_copy(f_t1[:], pb[3][:, :])
                    SC.wait_ge(cV.h, cV.n)
                    SC.activation(f_t2[:], f_t1[:], AF.Exp, bias=dtbias_pk[:])
                    SC.wait_ge(cA.h, cA.n)
                    SC.activation(f_dts[:], f_t2[:], AF.Ln, bias=onecol[:])
                    SC.wait_ge(cA.h, cA.n)
                    SC.activation(sm[:, 11:12], alog_pk[:], AF.Exp)
                    VE.wait_ge(cA.h, cA.n)
                    VE.tensor_scalar(sm[:, 12:13], sm[:, 11:12], -1.0, None, ALU.mult)
                    VE.tensor_scalar(f_A[:], f_dts[:], sm[:, 12:13], None, ALU.mult)
                    SC.wait_ge(cV.h, cV.n)
                    SC.activation(f_alpha[:], f_A[:], AF.Exp)
                    VE.wait_ge(cA.h, cA.n)
                    VE.tensor_tensor_scan(f_acum[:], fmc[:, 0:512], f_A[:], 0.0,
                                          ALU.mult, ALU.add)
                    VE.tensor_mul(f_t1[:], f_alpha[:], fmc[:, 0:512])
                    VE.tensor_tensor_scan(f_t2[:], f_t1[:], onesf[:], 0.0, ALU.mult, ALU.add)
                    VE.tensor_add(f_out[:], f_t2[:], fmc[:, 512:1024])
                    SY.wait_ge(cV.h, cV.n)
                    VE.tensor_copy(f_t1.ap().bitcast(BF16)[:, 0:512], f_out[:])
                    SY.wait_ge(cV.h, cV.n)
                    dma(SY, dS, sp_fm3[:, 0:512], f_t1.ap().bitcast(BF16)[:, 0:512])
                    SC.wait_ge(cV.h, cV.n)
                    SC.activation(f_out[:], f_acum[:], AF.Exp)
                    VE.wait_ge(cA.h, cA.n)
                    dS.wait(VE)
                    VE.tensor_copy(f_t1.ap().bitcast(BF16)[:, 0:512], f_out[:])
                    SY.wait_ge(cV.h, cV.n)
                    dma(SY, dS, sp_fm3[:, 512:1024], f_t1.ap().bitcast(BF16)[:, 0:512])
                    VE.scalar_tensor_tensor(f_t2[:], f_acum[:], -1.0,
                                            _bc(f_acum[:, 0:1], [(64, 8), (0, 64)], off=63),
                                            ALU.mult, ALU.add)
                    SC.wait_ge(cV.h, cV.n)
                    SC.activation(f_out[:], f_t2[:], AF.Exp)
                    VE.wait_ge(cA.h, cA.n)
                    dS.wait(VE)
                    VE.tensor_copy(f_t1.ap().bitcast(BF16)[:, 0:512], f_out[:])
                    SY.wait_ge(cV.h, cV.n)
                    dma(SY, dS, sp_fm3[:, 1024:1536], f_t1.ap().bitcast(BF16)[:, 0:512])
                    VE.tensor_copy(atot[:], _bc(f_acum[:, 0:1], [(64, 8)], off=63))
                    VE.tensor_tensor_scan(at2[:], atot[:], onesf[:, 0:8], 0.0,
                                          ALU.add, ALU.mult)
                    PE.wait_ge(cV.h, cV.n)
                    PE.matmul(pb[6][:, 0:1], shiftsum[:], at2[:, 7:8], start=True, stop=True)
                    cP.inc(PE.last)
                    VE.wait_ge(cP.h, cP.n)
                    VE.tensor_scalar(at2[:], at2[:], pb[6][:, 0:1], None, ALU.add)
                    SC.wait_ge(cV.h, cV.n)
                    dS.wait(SC)
                    SC.activation(f_t1[:, 0:8], atot[:], AF.Exp)
                    SC.activation(f_t2[:, 0:8], at2[:], AF.Exp)
                    SY.wait_ge(cA.h, cA.n)
                    dma(SY, dS, sp_dec, f_t1[:, 0:8])
                    dma(SY, dS, sp_pc, f_t2[:, 0:8])
                    dS.wait(SY)
                    for i3, dst in enumerate((lsum_tm, expa_tm, ds_tm)):
                        for r in range(4):
                            dma(SY, dS, f_t1.ap().bitcast(BF16)[:, 0:128],
                                sp_fm3[:, i3 * 512 + r * 128:i3 * 512 + (r + 1) * 128],
                                transpose=True)
                            dS.wait(VE)
                            for q in range(4):
                                tb = q * 4 + r
                                VE.tensor_copy(dst[:, tb * NH:(tb + 1) * NH],
                                               f_t1.ap().bitcast(BF16)[:, q * 32:(q + 1) * 32])
                            SY.wait_ge(cV.h, cV.n)
                    import dataclasses as _dc
                    for i3, (sp_src, dst) in enumerate(((sp_dec, dec_bc), (sp_pc, pc_bc))):
                        for par in range(2):
                            srcap = _dc.replace(
                                sp_src, offset=sp_src.offset + par * 8,
                                ap=[[16, 1], [16, 16], [256, 4], [1, 8]])
                            dstap = _dc.replace(
                                f_t2[0:1, 0:512],
                                ap=[list(f_t2[0:1, 0:512].ap[0]),
                                    [32, 16], [8, 4], [1, 8]])
                            SY.wait_ge(cP.h, cP.n)
                            dma(SY, dS, dstap, srcap)
                            dS.wait(PE)
                            PE.wait_ge(cV.h, cV.n)
                            PE.matmul(pb[6 + par][:, :], onesrow[:], f_t2[0:1, 0:512],
                                      start=True, stop=True)
                            cP.inc(PE.last)
                            VE.wait_ge(cP.h, cP.n)
                            VE.tensor_copy(dst[par * 64:(par + 1) * 64, :],
                                           pb[6 + par][par * 64:(par + 1) * 64, :])
                    PE.wait_ge(cV.h, cV.n)

                    PE.wait_ge(cV.h, cV.n)

                    for tb in range(NB):
                        sl = slice(tb * 128, (tb + 1) * 128)
                        SY.wait_ge(cP.h, cP.n)
                        for dc in range(8):
                            dma(SY, dS, hlo[:, dc * 512:dc * 512 + 128],
                                sp_h1lo[sl, dc * 128:(dc + 1) * 128], transpose=True)
                        dS.wait(PE)
                        PE.wait_ge(cV.h, cV.n)
                        PE.wait_ge(cA.h, cA.n)
                        for si, (c0, c1) in enumerate([(0, 512), (512, 1024), (1024, 1056)]):
                            for dc in range(8):
                                for hop, wop, fst, lst in (
                                        (hfm[:, dc * T + tb * 128:dc * T + (tb + 1) * 128],
                                         waux, dc == 0, False),
                                        (hfm[:, dc * T + tb * 128:dc * T + (tb + 1) * 128],
                                         wauxlo, False, False),
                                        (hlo[:, dc * 512:dc * 512 + 128],
                                         waux, False, dc == 7)):
                                    PE.matmul(pb[si][:, 0:c1 - c0], hop,
                                              wop[:, dc * AUXC + c0:dc * AUXC + c1],
                                              start=fst, stop=lst)
                        cP.inc(PE.last)
                        VE.wait_ge(cP.h, cP.n)
                        VE.tensor_add(sm2[:], pb[0][:, 0:NH], dtbias_tm[:])
                        SC.wait_ge(cV.h, cV.n)
                        SC.activation(sm3[:], sm2[:], AF.Exp)
                        SC.wait_ge(cA.h, cA.n)
                        SC.activation(dts_tm[:, tb * NH:(tb + 1) * NH], sm3[:], AF.Ln,
                                      bias=onecol[:])
                        VE.wait_ge(cA.h, cA.n)
                        dsl = dts_tm[:, tb * NH:(tb + 1) * NH]
                        VE.tensor_tensor(pP[:, 0:480], pb[0][:, 32:512],
                                         _bc(dsl[:, 0:1], [(1, 15), (0, 32)]), ALU.mult)
                        VE.tensor_tensor(pP[:, 480:992], pb[1][:, 0:512],
                                         _bc(dsl[:, 0:1], [(1, 16), (0, 32)], off=15), ALU.mult)
                        VE.tensor_tensor(pP[:, 992:1024], pb[2][:, 0:32],
                                         _bc(dsl[:, 0:1], [(1, 1), (0, 32)], off=31), ALU.mult)
                        PE.wait_ge(cV.h, cV.n)
                        PE.matmul(pb[4][0:NH, :], turn32[:], pP[:, 0:512],
                                  start=(tb == 0), stop=(tb == NB - 1))
                        PE.matmul(pb[5][0:NH, :], turn32[:], pP[:, 512:1024],
                                  start=(tb == 0), stop=(tb == NB - 1))
                        cP.inc(PE.last)
                        SY.wait_ge(cV.h, cV.n)
                        dma(SY, dS, sp_p0[sl, :], pP[:])
                        dS.wait(SY)
                        dS.wait(VE)

                    SC.activation(sm2[0:1, :], alog_row[0:1, :], AF.Exp)
                    PE.wait_ge(cA.h, cA.n)
                    PE.matmul(pb[3][:, 0:NH], onesrow[:], sm2[0:1, :], start=True, stop=True)
                    cP.inc(PE.last)
                    VE.wait_ge(cP.h, cP.n)
                    VE.tensor_scalar(sm3[:], pb[3][:, 0:NH], -1.0, None, ALU.mult)
                    for tb4 in range(NB):
                        gsl = slice(tb4 * NH, (tb4 + 1) * NH)
                        VE.tensor_mul(gam_tm[:, gsl], dts_tm[:, gsl], sm3[:])
                    SC.wait_ge(cV.h, cV.n)
                    SC.activation(lsx[:], gam_tm[:], AF.Exp)
                    VE.wait_ge(cA.h, cA.n)
                    VE.tensor_scalar(f_t1[:], gam_tm[:], 1e-6, None, ALU.add)
                    VE.reciprocal(f_t2[:], f_t1[:])
                    VE.tensor_scalar(f_t1[:], lsx[:], 1.0, None, ALU.subtract)
                    VE.tensor_mul(lsx[:], f_t1[:], f_t2[:])
                    VE.tensor_scalar(gam_tm[:], lsx[:], 0.5, 1.0, ALU.mult, ALU.add)

                    VE.wait_ge(cP.h, cP.n)
                    VE.tensor_copy(ccrecv[0:1, 0:512], pb[4][0:1, :])
                    VE.tensor_copy(ccrecv[0:1, 512:1024], pb[5][0:1, :])
                    GP.wait_ge(cV.h, cV.n)
                    dma(GP, dG, cc1i, ccrecv[:])
                    dG.wait(GP)
                    i = GP.collective_compute(
                        "AllGather", ALU.bypass,
                        replica_groups=[[0, 1], [2, 3], [4, 5], [6, 7]],
                        ins=[cc1i.opt()], outs=[cc1o.opt()])
                    ccs.inc(i)
                    GP.wait_ge(ccs.h, ccs.n)
                    dma(GP, dG, ccrecv[:], cc1o[0:1, :])
                    dG.wait(VE)
                    VE.tensor_scalar(offrow[:], ccrecv[:], maskc[0:1, 0:1], None, ALU.mult)

                    rR = xa
                    kF = xb
                    kI = na[:].bitcast(I32)
                    for tb in range(NB):
                        sl = slice(tb * 128, (tb + 1) * 128)
                        SY.wait_ge(cV.h, cV.n)
                        dma(SY, dS, pP[:], sp_p0[sl, :])
                        dS.wait(PE)
                        PE.wait_ge(cV.h, cV.n)
                        for cs in range(2):
                            bank = pb[4 + cs]
                            PE.matmul(bank[:, :], tri[:], pP[:, cs * 512:(cs + 1) * 512],
                                      start=True, stop=False)
                            PE.matmul(bank[:, :], onesrow[:],
                                      offrow[0:1, cs * 512:(cs + 1) * 512],
                                      start=False, stop=True)
                            PE.matmul(pb[6 + cs][0:NH, :], turn32[:],
                                      pP[:, cs * 512:(cs + 1) * 512], start=True, stop=True)
                        cP.inc(PE.last)
                        VE.wait_ge(cP.h, cP.n)
                        for cs in range(2):
                            c0 = cs * 512
                            VE.tensor_add(offrow[0:1, c0:c0 + 512],
                                          offrow[0:1, c0:c0 + 512], pb[6 + cs][0:1, :])
                        def wrap_turns(c0, src_ap, shift):
                            c1 = c0 + 512
                            VE.wait_ge(cA.h, cA.n)
                            if shift:
                                VE.tensor_scalar(kI[:, c0:c1], src_ap, shift, None, ALU.add)
                                VE.tensor_scalar(rR[:, c0:c1], src_ap, shift, None, ALU.add)
                            else:
                                VE.tensor_scalar(kI[:, c0:c1], src_ap, 1.0, None, ALU.mult)
                                VE.tensor_copy(rR[:, c0:c1], src_ap)
                            VE.tensor_copy(kF[:, c0:c1], kI[:, c0:c1])
                            VE.tensor_sub(rR[:, c0:c1], rR[:, c0:c1], kF[:, c0:c1])
                            VE.tensor_scalar(kF[:, c0:c1], rR[:, c0:c1], 0.5, None, ALU.is_gt)
                            VE.tensor_sub(rR[:, c0:c1], rR[:, c0:c1], kF[:, c0:c1])
                            VE.tensor_scalar(kF[:, c0:c1], rR[:, c0:c1], -0.5, None, ALU.is_lt)
                            VE.tensor_add(rR[:, c0:c1], rR[:, c0:c1], kF[:, c0:c1])

                        for cs in range(2):
                            wrap_turns(cs * 512, pb[4 + cs][:, :], 0.0)
                        SC.wait_ge(cV.h, cV.n)
                        SC.activation(sinb[:], rR[:], AF.Sin, scale=2 * PI)
                        for cs in range(2):
                            wrap_turns(cs * 512, pb[4 + cs][:, :], 0.25)
                        SC.wait_ge(cV.h, cV.n)
                        SC.activation(cosb[:], rR[:], AF.Sin, scale=2 * PI)
                        SY.wait_ge(cA.h, cA.n)
                        dma(SY, dS, sp_cs[sl, 0:D], sinb[:])
                        dma(SY, dS, sp_cs[sl, D:2 * D], cosb[:])
                        dS.wait(VE)
                    fence()

            if 'p1' in phases:
                with contextlib.ExitStack() as _px:
                    _px.enter_context(nc.named_scope('p1'))
                    wsA = _px.enter_context(nc.sbuf_tensor("p1_wsa", [128, 8 * 512], BF16))
                    wsB = _px.enter_context(nc.sbuf_tensor("p1_wsb", [128, 8 * 512], BF16))
                    u_t = _px.enter_context(nc.sbuf_tensor("p1_ut", [128, TBT * DI], BF16))
                    bds_a = _px.enter_context(nc.sbuf_tensor("p1_bdsa", [128, DI], BF16))
                    bds_b = _px.enter_context(nc.sbuf_tensor("p1_bdsb", [128, DI], BF16))
                    bhat = _px.enter_context(nc.sbuf_tensor("p1_bhat", [128, 32 * 128], BF16))
                    chat = _px.enter_context(nc.sbuf_tensor("p1_chat", [128, 32 * 128], BF16))
                    g2a = _px.enter_context(nc.sbuf_tensor("p1_g2a", [128, NH * 64], BF16))
                    g2b = _px.enter_context(nc.sbuf_tensor("p1_g2b", [128, NH * 64], BF16))
                    fs_sb = _px.enter_context(nc.sbuf_tensor("p1_fs", [128, 8 * 1024], F32))
                    csin = _px.enter_context(nc.sbuf_tensor("p1_sin", [128, TBT * D], BF16))
                    ccos = _px.enter_context(nc.sbuf_tensor("p1_cos", [128, TBT * D], BF16))
                    st1 = _px.enter_context(nc.sbuf_tensor("p1_st1", [128, 512], BF16))
                    st2 = _px.enter_context(nc.sbuf_tensor("p1_st2", [128, 512], BF16))
                    st3 = _px.enter_context(nc.sbuf_tensor("p1_st3", [128, 512], BF16))
                    bnw = _px.enter_context(nc.sbuf_tensor("p1_bnw", [128, DS], BF16))
                    cnw = _px.enter_context(nc.sbuf_tensor("p1_cnw", [128, DS], BF16))
                    bbias = _px.enter_context(nc.sbuf_tensor("p1_bb", [128, DI], BF16))
                    cbias = _px.enter_context(nc.sbuf_tensor("p1_cb", [128, DI], BF16))
                    dparam = _px.enter_context(nc.sbuf_tensor("p1_dp", [128, NH], F32))
                    sf1 = _px.enter_context(nc.sbuf_tensor("p1_sf1", [128, 512], F32))
                    VE.memset(u_t[:, 0:2 * DI], 0.0)
                    VE.memset(g2a[64:128, :], 0.0)
                    VE.memset(g2b[0:64, :], 0.0)
                    VE.memset(bds_a[64:128, :], 0.0)
                    VE.memset(bds_b[0:64, :], 0.0)
                    dS.wait(VE)
                    SY.wait_ge(cV.h, cV.n)
                    for sp in (sp_br, sp_c1, sp_c2):
                        for tb in range(NB):
                            dma(SY, dS, sp[tb * 128:(tb + 1) * 128, :], u_t[:, 0:2 * DI])
                    dS.wait(VE)
                    dma(SY, dS, bnw[:], bnw_c)
                    dma(SY, dS, cnw[:], cnw_c)
                    dma(SY, dS, bbias[:], bbias_c)
                    dma(SY, dS, cbias[:], cbias_c)
                    dma(SY, dS, dparam[:], dparam_c)

                    for ti in range(NT):
                        t0 = ti * 512
                        if p1grp >= 3:
                            SY.wait_ge(cV.h, cV.n)
                            for tb4 in range(TBT):
                                sl = slice(t0 + tb4 * 128, t0 + (tb4 + 1) * 128)
                                dma(SY, dS, csin[:, tb4 * D:(tb4 + 1) * D], sp_cs[sl, 0:D])
                                dma(SY, dS, ccos[:, tb4 * D:(tb4 + 1) * D], sp_cs[sl, D:2 * D])

                        for cs in range(16):
                            wbuf = wsA if cs % 2 == 0 else wsB
                            SY.wait_ge(cP.h, cP.n)
                            dma(SY, dS, wbuf.ap().rearrange("p (c n) -> p c n", c=8),
                                sp_win.rearrange("(c p) n -> p c n", p=128)
                                [:, :, cs * 512:(cs + 1) * 512])
                            gcol = cs * 512
                            for tb4 in range(TBT):
                                tb = ti * TBT + tb4
                                sl = slice(tb * 128, (tb + 1) * 128)
                                bank = pb[(cs * TBT + tb4) % 2]
                                dS.wait(PE)
                                PE.wait_ge(cV.h, cV.n)
                                PE.wait_ge(cA.h, cA.n)
                                dA.wait(PE)
                                for dc in range(8):
                                    PE.matmul(bank[:, :],
                                              hfm[:, dc * T + tb * 128:dc * T + (tb + 1) * 128],
                                              wbuf[:, dc * 512:(dc + 1) * 512],
                                              start=(dc == 0), stop=(dc == 7))
                                cP.inc(PE.last)
                                if gcol < DI:
                                    SC.wait_ge(cP.h, cP.n)
                                    dA.wait(SC)
                                    SC.wait_ge(cV.h, cV.n)
                                    SC.activation(st1[:, :], bank[:, :], AF.Silu)
                                    SC.wait_ge(cA.h, cA.n)
                                    dma(SC, dA, sp_zs[sl, gcol:gcol + 512], st1[:, :])
                                elif gcol < 2 * DI:
                                    lc = gcol - DI
                                    h0 = lc // 64
                                    VE.wait_ge(cP.h, cP.n)
                                    gsl = gam_tm[:, tb * NH + h0:tb * NH + h0 + 1]
                                    VE.tensor_tensor(
                                        u_t[:, tb4 * DI + lc:tb4 * DI + lc + 512],
                                        bank[:, :], _bc(gsl, [(1, 8), (0, 64)]), ALU.mult)
                                    VE.tensor_reduce(
                                        sm2[:, h0:h0 + 8],
                                        bank.ap().rearrange("p (h k) -> p h k", h=8),
                                        AX.X, ALU.add)
                                    VE.tensor_tensor(skip_t[:, tb4 * NH + h0:tb4 * NH + h0 + 8],
                                                     sm2[:, h0:h0 + 8],
                                                     dparam[:, h0:h0 + 8], ALU.mult)
                                else:
                                    isB = gcol < CO
                                    lc = gcol - (BO if isB else CO)
                                    h0 = lc // 64
                                    SC.wait_ge(cP.h, cP.n)
                                    for hh in range(8):
                                        SC.activation(sf1[:, hh * 64:(hh + 1) * 64],
                                                      bank[:, hh * 64:(hh + 1) * 64],
                                                      AF.Square,
                                                      accum_out=sm2[:, h0 + hh:h0 + hh + 1])
                                    VE.wait_ge(cA.h, cA.n)
                                    VE.tensor_scalar(sm3[:, h0:h0 + 8], sm2[:, h0:h0 + 8],
                                                     1.0 / DS, 1e-5, ALU.mult, ALU.add)
                                    SC.wait_ge(cV.h, cV.n)
                                    SC.activation(sm2[:, h0:h0 + 8], sm3[:, h0:h0 + 8], AF.Sqrt)
                                    VE.wait_ge(cA.h, cA.n)
                                    dA.wait(VE)
                                    VE.reciprocal(sm3[:, h0:h0 + 8], sm2[:, h0:h0 + 8])
                                    wt = bnw if isB else cnw
                                    bt = bbias if isB else cbias
                                    VE.tensor_tensor(st1[:, :], bank[:, :],
                                                     _bc(sm3[:, h0:h0 + 1], [(1, 8), (0, 64)]),
                                                     ALU.mult)
                                    VE.tensor_tensor(st1[:, :], st1[:, :],
                                                     _bc(wt[:, 0:1], [(0, 8), (1, 64)]),
                                                     ALU.mult)
                                    VE.tensor_add(st1[:, :], st1[:, :], bt[:, lc:lc + 512])
                                    ev = _bc(st1[:, 0:1], [(64, 8), (1, 32)])
                                    od = _bc(st1[:, 0:1], [(64, 8), (1, 32)], off=32)
                                    aS = _bc(csin[:, 0:1], [(32, 8), (1, 32)],
                                             off=tb4 * D + h0 * 32)
                                    aC = _bc(ccos[:, 0:1], [(32, 8), (1, 32)],
                                             off=tb4 * D + h0 * 32)
                                    rev = _bc(st2[:, 0:1], [(64, 8), (1, 32)])
                                    rod = _bc(st2[:, 0:1], [(64, 8), (1, 32)], off=32)
                                    m1 = _bc(st3[:, 0:1], [(64, 8), (1, 32)])
                                    m2_ = _bc(st3[:, 0:1], [(64, 8), (1, 32)], off=32)
                                    VE.tensor_tensor(m1, aC, ev, ALU.mult)
                                    VE.tensor_tensor(m2_, aS, od, ALU.mult)
                                    VE.tensor_tensor(rev, m1, m2_, ALU.subtract)
                                    VE.tensor_tensor(m1, aS, ev, ALU.mult)
                                    VE.tensor_tensor(m2_, aC, od, ALU.mult)
                                    VE.tensor_tensor(rod, m1, m2_, ALU.add)
                                    if isB:
                                        SY.wait_ge(cV.h, cV.n)
                                        for par in range(2):
                                            dma(SY, dS,
                                                _bc(sp_br[sl, (h0 + par) * 128 + par * 64:],
                                                    [(256, 4), (1, 64)]),
                                                _bc(st2[:, par * 64:], [(128, 4), (1, 64)]))
                                    else:
                                        VE.tensor_tensor(
                                            st3[:, :], st2[:, :],
                                            _bc(lsum_tm[:, tb * NH + h0:tb * NH + h0 + 1],
                                                [(1, 8), (0, 64)]), ALU.mult)
                                        SY.wait_ge(cV.h, cV.n)
                                        for par in range(2):
                                            dma(SY, dS,
                                                _bc(sp_c1[sl, (h0 + par) * 128 + par * 64:],
                                                    [(256, 4), (1, 64)]),
                                                _bc(st3[:, par * 64:], [(128, 4), (1, 64)]))
                                        dS.wait(VE)
                                        VE.tensor_tensor(
                                            st3[:, :], st2[:, :],
                                            _bc(expa_tm[:, tb * NH + h0:tb * NH + h0 + 1],
                                                [(1, 8), (0, 64)]), ALU.mult)
                                        SY.wait_ge(cV.h, cV.n)
                                        for par in range(2):
                                            dma(SY, dS,
                                                _bc(sp_c2[sl, (h0 + par) * 128 + par * 64:],
                                                    [(256, 4), (1, 64)]),
                                                _bc(st3[:, par * 64:], [(128, 4), (1, 64)]))
                                    dS.wait(VE)

                        for tb4 in range(TBT):
                            tb = ti * TBT + tb4
                            sl = slice(tb * 128, (tb + 1) * 128)
                            if p1cut < 2:
                                continue
                            SY.wait_ge(cP.h, cP.n)
                            SY.wait_ge(cV.h, cV.n)
                            dS.wait(SY)
                            dA.wait(SY)
                            dma(SY, dS, bhat[:], sp_br[sl, :])
                            SC.wait_ge(cP.h, cP.n)
                            SC.wait_ge(cV.h, cV.n)
                            dma(SC, dA, chat[:], sp_c1[sl, :])
                            for par in range(2):
                                bdst = bds_a if par == 0 else bds_b
                                rsl = slice(par * 64, par * 64 + 64)
                                dma(SY, dS, bdst[rsl, 0:DI],
                                    _bc(sp_br[tb * 128 + par * 64:tb * 128 + par * 64 + 64,
                                              0:1],
                                        [(256, 16), (192, 2), (1, 64)]))
                            dS.wait(VE)
                            VE.wait_ge(cP.h, cP.n)
                            for par in range(2):
                                bdst = bds_a if par == 0 else bds_b
                                rsl = slice(par * 64, par * 64 + 64)
                                VE.tensor_tensor(
                                    bdst[rsl, :], bdst[rsl, :],
                                    _bc(ds_tm[rsl, tb * NH:tb * NH + 1],
                                        [(1, 32), (0, 64)]), ALU.mult)
                            dS.wait(PE)
                            dA.wait(PE)
                            trwave([(bhat[:, h * 128:(h + 1) * 128],
                                     bhat[:, h * 128:(h + 1) * 128]) for h in range(32)])
                            trwave([(chat[:, h * 128:(h + 1) * 128],
                                     chat[:, h * 128:(h + 1) * 128]) for h in range(32)])
                            for hg in range(4 if p1cut >= 3 else 0):
                                bank = pb[2 + hg % 2]
                                PE.wait_ge(cV.h, cV.n)
                                for hh in range(8):
                                    h = hg * 8 + hh
                                    for c2 in range(2):
                                        PE.matmul(
                                            bank[64 * c2:64 * c2 + 64, hh * 64:(hh + 1) * 64],
                                            bhat[:, h * 128 + c2 * 64:h * 128 + c2 * 64 + 64],
                                            chat[:, h * 128 + c2 * 64:h * 128 + c2 * 64 + 64],
                                            start=True, stop=True)
                                cP.inc(PE.last)
                                VE.wait_ge(cP.h, cP.n)
                                VE.tensor_copy(g2a[0:64, hg * 512:(hg + 1) * 512], bank[0:64, :])
                                VE.tensor_copy(g2b[64:128, hg * 512:(hg + 1) * 512],
                                               bank[64:128, :])
                            for hg in range(4 if p1cut >= 4 else 0):
                                ybank = pb[4 + hg % 2]
                                PE.wait_ge(cV.h, cV.n)
                                for hh in range(8):
                                    h = hg * 8 + hh
                                    for c2 in range(2):
                                        PE.matmul(
                                            ybank[64 * c2:64 * c2 + 64, hh * 64:(hh + 1) * 64],
                                            (g2a if c2 == 0 else g2b)[:, hg * 512 + hh * 64:
                                                                      hg * 512 + (hh + 1) * 64],
                                            u_t[:, tb4 * DI + h * 64:tb4 * DI + (h + 1) * 64],
                                            start=True, stop=True)
                                cP.inc(PE.last)
                                VE.wait_ge(cP.h, cP.n)
                                ssl = skip_t[:, tb4 * NH + hg * 8:tb4 * NH + hg * 8 + 1]
                                VE.tensor_tensor(st1[:, :], ybank[:, :],
                                                 _bc(ssl, [(1, 8), (0, 64)]), ALU.add)
                                SY.wait_ge(cV.h, cV.n)
                                dma(SY, dS, sp_y1[sl, hg * 512:(hg + 1) * 512], st1[:, :])
                                dS.wait(VE)
                            for c2 in range(2 if p1cut >= 5 else 0):
                                c = tb * 2 + c2
                                cit = tb4 * 2 + c2
                                PE.wait_ge(cV.h, cV.n)
                                for h in range(NH):
                                    g = h // 2
                                    sbank = pb[g // 8]
                                    PE.matmul(
                                        sbank[(h % 2) * 64:(h % 2) * 64 + 64,
                                              (g % 8) * 64:(g % 8 + 1) * 64],
                                        (bds_a if c2 == 0 else bds_b)
                                        [:, h * 64:(h + 1) * 64],
                                        u_t[:, tb4 * DI + h * 64:tb4 * DI + (h + 1) * 64],
                                        start=True, stop=True)
                                cP.inc(PE.last)
                                VE.wait_ge(cP.h, cP.n)
                                for g in range(16):
                                    sbank = pb[g // 8]
                                    prev = (fslast[:, g * 64:(g + 1) * 64] if cit == 0 else
                                            fs_sb[:, (cit - 1) * 1024 + g * 64:
                                                  (cit - 1) * 1024 + (g + 1) * 64])
                                    VE.scalar_tensor_tensor(
                                        fs_sb[:, cit * 1024 + g * 64:cit * 1024 + (g + 1) * 64],
                                        prev, dec_bc[:, g * NCH + c:g * NCH + c + 1],
                                        sbank[:, (g % 8) * 64:(g % 8 + 1) * 64],
                                        ALU.mult, ALU.add)
                        if p1cut < 5:
                            continue
                        VE.tensor_copy(fslast[:, :], fs_sb[:, 7 * 1024:8 * 1024])
                        SY.wait_ge(cV.h, cV.n)
                        dma(SY, dS, sp_fs[ti], fs_sb[:])
                        dS.wait(SY)
                        dS.wait(VE)

                    if p1cut >= 4:
                        GP.wait_ge(cV.h, cV.n)
                        dma(GP, dG, cc2i, fslast[:])
                        dG.wait(GP)
                        i = GP.collective_compute(
                            "AllGather", ALU.bypass,
                            replica_groups=[[0, 1], [2, 3], [4, 5], [6, 7]],
                            ins=[cc2i.opt()], outs=[cc2o.opt()])
                        ccs.inc(i)
                        GP.wait_ge(ccs.h, ccs.n)
                        dma(GP, dG, carry_rm[:], cc2o[0])
                        dG.wait(VE)
                        VE.tensor_scalar(carry_rm[:], carry_rm[:], maskc[:], None, ALU.mult)
                    fence()

        if 'post' in phases:
            with contextlib.ExitStack() as _px:
                _px.enter_context(nc.named_scope('post'))
                woutr = _px.enter_context(nc.sbuf_tensor("pp_wout", [128, 16 * D], BF16))
                fsl = _px.enter_context(nc.sbuf_tensor("pp_fsl", [128, 8 * 1024], F32))
                fsg = _px.enter_context(nc.sbuf_tensor("pp_fsg", [128, 8 * 1024], BF16))
                chat2 = _px.enter_context(nc.sbuf_tensor("pp_ch2", [128, 32 * 128], BF16))
                y1b = _px.enter_context(nc.sbuf_tensor("pp_y1", [128, DI], BF16))
                zsb = _px.enter_context(nc.sbuf_tensor("pp_zs", [128, DI], BF16))
                yb = _px.enter_context(nc.sbuf_tensor("pp_yb", [128, DI], BF16))
                yfm = _px.enter_context(nc.sbuf_tensor("pp_yfm", [128, 16 * 512], BF16))
                xnb = _px.enter_context(nc.sbuf_tensor("pp_xn", [128, D], F32))
                s0b = _px.enter_context(nc.sbuf_tensor("pp_s0", [128, D], F32))
                r1b = _px.enter_context(nc.sbuf_tensor("pp_r1", [128, D], F32))
                sxb = _px.enter_context(nc.sbuf_tensor("pp_sx", [128, D], F32))
                sf1 = _px.enter_context(nc.sbuf_tensor("pp_sf", [128, 512], F32))
                h2b = _px.enter_context(nc.sbuf_tensor("pp_h2", [128, D], BF16))
                n2w = _px.enter_context(nc.sbuf_tensor("pp_n2w", [128, D], F32))
                dma(SY, dS, n2w[:], n2w_c)
                dma(GP, dG, woutr.ap().rearrange("p (c n) -> p c n", c=16),
                    sp_wout.rearrange("(c p) n -> p c n", p=128))
                for ti in range(NT):
                    t0 = ti * 512
                    SY.wait_ge(cV.h, cV.n)
                    dma(SY, dS, fsl[:], sp_fs[ti])
                    dS.wait(VE)
                    for cit in range(8):
                        c = ti * 8 + cit
                        for g in range(16):
                            VE.scalar_tensor_tensor(
                                fsg[:, cit * 1024 + g * 64:cit * 1024 + (g + 1) * 64],
                                carry_rm[:, g * 64:(g + 1) * 64],
                                pc_bc[:, g * NCH + c:g * NCH + c + 1],
                                fsl[:, cit * 1024 + g * 64:cit * 1024 + (g + 1) * 64],
                                ALU.mult, ALU.add)
                    for tb4 in range(TBT):
                        tb = ti * TBT + tb4
                        sl = slice(tb * 128, (tb + 1) * 128)
                        SY.wait_ge(cP.h, cP.n)
                        SY.wait_ge(cV.h, cV.n)
                        dma(SY, dS, chat2[:], sp_c2[sl, :])
                        dma(SY, dS, y1b[:], sp_y1[sl, :])
                        dma(SY, dS, zsb[:], sp_zs[sl, :])
                        dma(SY, dS, xnb[:], sp_xn[sl, :])
                        dma(SY, dS, s0b[:], sp_s0[sl, :])
                        dS.wait(PE)
                        trwave([(chat2[:, h * 128:(h + 1) * 128],
                                 chat2[:, h * 128:(h + 1) * 128]) for h in range(32)])
                        PE.wait_ge(cV.h, cV.n)
                        for hg in range(4):
                            bank = pb[2 + hg % 2]
                            PE.wait_ge(cV.h, cV.n)
                            for hh in range(8):
                                h = hg * 8 + hh
                                g = h // 2
                                for c2 in range(2):
                                    cit = tb4 * 2 + c2
                                    PE.matmul(
                                        bank[64 * c2:64 * c2 + 64, hh * 64:(hh + 1) * 64],
                                        chat2[:, h * 128 + c2 * 64:h * 128 + c2 * 64 + 64],
                                        fsg[:, cit * 1024 + g * 64:cit * 1024 + (g + 1) * 64],
                                        start=True, stop=True)
                            cP.inc(PE.last)
                            VE.wait_ge(cP.h, cP.n)
                            VE.tensor_tensor(sf1[:, :], bank[:, :],
                                             y1b[:, hg * 512:(hg + 1) * 512], ALU.add)
                            VE.tensor_tensor(yb[:, hg * 512:(hg + 1) * 512], sf1[:, :],
                                             zsb[:, hg * 512:(hg + 1) * 512], ALU.mult)
                        SY.wait_ge(cV.h, cV.n)
                        dma(SY, dS, sp_y[sl, :], yb[:])
                        trwave([(yb[:, dc * 128:(dc + 1) * 128],
                                 yfm[:, dc * 512 + tb4 * 128:dc * 512 + (tb4 + 1) * 128])
                                for dc in range(16)])
                    dS.wait(SY)
                    for tb4 in range(TBT):
                        tb = ti * TBT + tb4
                        sl = slice(tb * 128, (tb + 1) * 128)
                        dS.wait(PE)
                        dG.wait(PE)
                        PE.wait_ge(cA.h, cA.n)
                        PE.wait_ge(cV.h, cV.n)
                        for dc in range(16):
                            PE.matmul(pb[4][:, :],
                                      yfm[:, dc * 512 + tb4 * 128:dc * 512 + (tb4 + 1) * 128],
                                      woutr[:, dc * D:dc * D + 512],
                                      start=(dc == 0), stop=(dc == 15))
                            PE.matmul(pb[5][:, :],
                                      yfm[:, dc * 512 + tb4 * 128:dc * 512 + (tb4 + 1) * 128],
                                      woutr[:, dc * D + 512:(dc + 1) * D],
                                      start=(dc == 0), stop=(dc == 15))
                        cP.inc(PE.last)
                        VE.wait_ge(cP.h, cP.n)
                        for half in range(2):
                            bank = pb[4 + half]
                            hsl = slice(half * 512, (half + 1) * 512)
                            VE.tensor_tensor(r1b[:, hsl], bank[:, :], xnb[:, hsl], ALU.add)
                            VE.tensor_tensor(sxb[:, hsl], bank[:, :], s0b[:, hsl], ALU.add)
                        SY.wait_ge(cV.h, cV.n)
                        dma(SY, dS, sp_sx[sl, :], sxb[:])
                        SC.wait_ge(cV.h, cV.n)
                        SC.activation(sf1[:, :], r1b[:, 0:512], AF.Square,
                                      accum_out=sm[:, 0:1])
                        SC.activation(sf1[:, :], r1b[:, 512:1024], AF.Square,
                                      accum_out=sm[:, 1:2])
                        VE.wait_ge(cA.h, cA.n)
                        VE.tensor_add(sm[:, 2:3], sm[:, 0:1], sm[:, 1:2])
                        VE.tensor_scalar(sm[:, 3:4], sm[:, 2:3], 1.0 / D, 1e-5,
                                         ALU.mult, ALU.add)
                        SC.wait_ge(cV.h, cV.n)
                        SC.activation(sm[:, 4:5], sm[:, 3:4], AF.Sqrt)
                        VE.wait_ge(cA.h, cA.n)
                        VE.reciprocal(sm[:, 5:6], sm[:, 4:5])
                        VE.scalar_tensor_tensor(h2b[:], r1b[:], sm[:, 5:6], n2w[:],
                                                ALU.mult, ALU.mult)
                        SY.wait_ge(cV.h, cV.n)
                        dma(SY, dS, sp_h2[sl, :], h2b[:])
                        dS.wait(SY)
                        dS.wait(VE)
                fence()

        if '2a' in phases:
            with contextlib.ExitStack() as _px:
                _px.enter_context(nc.named_scope('2a'))
                wgr = _px.enter_context(nc.sbuf_tensor("pa_wg", [128, 8 * DMLP], BF16))
                wur = _px.enter_context(nc.sbuf_tensor("pa_wu", [128, 8 * DMLP], BF16))
                wdr = _px.enter_context(nc.sbuf_tensor("pa_wd", [128, 20 * 1024], BF16))
                h2f = _px.enter_context(nc.sbuf_tensor("pa_h2f", [128, 8 * 128], BF16))
                gb = _px.enter_context(nc.sbuf_tensor("pa_g", [128, DMLP], BF16))
                gt = _px.enter_context(nc.sbuf_tensor("pa_gt", [128, DMLP], BF16))
                gf = _px.enter_context(nc.sbuf_tensor("pa_gf", [128, 20 * 128], BF16))
                sxb = _px.enter_context(nc.sbuf_tensor("pa_sx", [128, D], F32))
                acb = _px.enter_context(nc.sbuf_tensor("pa_ac", [128, D], F32))
                dma(GP, dG, wgr.ap().rearrange("p (c n) -> p c n", c=8),
                    sp_wg.rearrange("(c p) n -> p c n", p=128))
                dma(GP, dG, wur.ap().rearrange("p (c n) -> p c n", c=8),
                    sp_wu.rearrange("(c p) n -> p c n", p=128))
                dma(GP, dG, wdr.ap().rearrange("p (c n) -> p c n", c=20),
                    sp_wd.rearrange("(c p) n -> p c n", p=128))
                for tb in range(NB):
                    sl = slice(tb * 128, (tb + 1) * 128)
                    SY.wait_ge(cP.h, cP.n)
                    SY.wait_ge(cV.h, cV.n)
                    dma(SY, dS, h2f[:], sp_h2[sl, :])
                    dS.wait(PE)
                    dG.wait(PE)
                    trwave([(h2f[:, dc * 128:(dc + 1) * 128],
                             h2f[:, dc * 128:(dc + 1) * 128]) for dc in range(8)])
                    PE.wait_ge(cV.h, cV.n)
                    for s5 in range(5):
                        bank = pb[s5 % 2]
                        PE.wait_ge(cA.h, cA.n)
                        for dc in range(8):
                            PE.matmul(bank[:, :], h2f[:, dc * 128:(dc + 1) * 128],
                                      wgr[:, dc * DMLP + s5 * 512:dc * DMLP + (s5 + 1) * 512],
                                      start=(dc == 0), stop=(dc == 7))
                        cP.inc(PE.last)
                        SC.wait_ge(cP.h, cP.n)
                        SC.activation(gb[:, s5 * 512:(s5 + 1) * 512], bank[:, :], AF.Silu)
                    for s5 in range(5):
                        bank = pb[2 + s5 % 2]
                        PE.wait_ge(cV.h, cV.n)
                        for dc in range(8):
                            PE.matmul(bank[:, :], h2f[:, dc * 128:(dc + 1) * 128],
                                      wur[:, dc * DMLP + s5 * 512:dc * DMLP + (s5 + 1) * 512],
                                      start=(dc == 0), stop=(dc == 7))
                        cP.inc(PE.last)
                        VE.wait_ge(cP.h, cP.n)
                        VE.wait_ge(cA.h, cA.n)
                        VE.tensor_tensor(gt[:, s5 * 512:(s5 + 1) * 512], bank[:, :],
                                         gb[:, s5 * 512:(s5 + 1) * 512], ALU.mult)
                    SY.wait_ge(cV.h, cV.n)
                    dma(SY, dS, sp_gated[sl, :], gt[:])
                    trwave([(gt[:, kc * 128:(kc + 1) * 128],
                             gf[:, kc * 128:(kc + 1) * 128]) for kc in range(20)])
                    dma(SY, dS, sxb[:], sp_sx[sl, :])
                    dS.wait(PE)
                    PE.wait_ge(cV.h, cV.n)
                    for half in range(2):
                        bank = pb[4 + half]
                        for kc in range(20):
                            PE.matmul(bank[:, :], gf[:, kc * 128:(kc + 1) * 128],
                                      wdr[:, kc * 1024 + half * 512:kc * 1024 + (half + 1) * 512],
                                      start=(kc == 0), stop=(kc == 19))
                    cP.inc(PE.last)
                    VE.wait_ge(cP.h, cP.n)
                    VE.tensor_tensor(acb[:, 0:512], pb[4][:, :], sxb[:, 0:512], ALU.add)
                    VE.tensor_tensor(acb[:, 512:1024], pb[5][:, :], sxb[:, 512:1024], ALU.add)
                    SY.wait_ge(cV.h, cV.n)
                    dma(SY, dS, sp_acc[sl, :], acb[:])
                    dS.wait(SY)
                    dS.wait(VE)
                fence()

        if '2b' in phases:
            with contextlib.ExitStack() as _px:
                _px.enter_context(nc.named_scope('2b'))
                xnf = _px.enter_context(nc.sbuf_tensor("pf_xnf", [128, 8 * 512], BF16))
                wqa = _px.enter_context(nc.sbuf_tensor("pf_wqa", [128, 8 * 512], BF16))
                wqb = _px.enter_context(nc.sbuf_tensor("pf_wqb", [128, 8 * 512], BF16))
                hft = _px.enter_context(nc.sbuf_tensor("pf_hf", [128, TBT * HFF], BF16))
                mus = _px.enter_context(nc.sbuf_tensor("pf_mu", [128, TBT * 8], F32))
                m2s = _px.enter_context(nc.sbuf_tensor("pf_m2", [128, TBT * 8], F32))
                sf1 = _px.enter_context(nc.sbuf_tensor("pf_sf", [128, 512], F32))
                hnb = _px.enter_context(nc.sbuf_tensor("pf_hn", [128, HFF], BF16))
                hffnfm = _px.enter_context(nc.sbuf_tensor("pf_hfm", [128, 32 * 512], BF16))
                wp1 = _px.enter_context(nc.sbuf_tensor("pf_wp1", [128, 8 * 1024], BF16))
                wp2 = _px.enter_context(nc.sbuf_tensor("pf_wp2", [128, 8 * 1024], BF16))
                acb = _px.enter_context(nc.sbuf_tensor("pf_ac", [128, D], F32))
                outb = _px.enter_context(nc.sbuf_tensor("pf_out", [128, D], F32))
                for ti in range(NT):
                    t0 = ti * 512
                    SY.wait_ge(cP.h, cP.n)
                    SY.wait_ge(cV.h, cV.n)
                    dma(SY, dS, hffnfm.ap().rearrange("p (b c) -> p b c", b=16)
                        [:, 0:4, :],
                        sp_xn16.rearrange("(b p) c -> p b c", p=128)
                        [:, ti * 4:(ti + 1) * 4, :])
                    dS.wait(PE)
                    trwave([(hffnfm[:, b * 1024 + dc * 128:b * 1024 + (dc + 1) * 128],
                             xnf[:, dc * 512 + b * 128:dc * 512 + (b + 1) * 128])
                            for dc in range(8) for b in range(4)])
                    PE.wait_ge(cV.h, cV.n)
                    for cs in range(8):
                        wbuf = wqa if cs % 2 == 0 else wqb
                        SY.wait_ge(cP.h, cP.n)
                        dma(SY, dS, wbuf.ap().rearrange("p (c n) -> p c n", c=8),
                            sp_wqf.rearrange("(c p) n -> p c n", p=128)
                            [:, :, cs * 512:(cs + 1) * 512])
                        for tb4 in range(TBT):
                            bank = pb[(cs * TBT + tb4) % 2]
                            dS.wait(PE)
                            PE.wait_ge(cA.h, cA.n)
                            for dc in range(8):
                                PE.matmul(bank[:, :],
                                          xnf[:, dc * 512 + tb4 * 128:dc * 512 + (tb4 + 1) * 128],
                                          wbuf[:, dc * 512:(dc + 1) * 512],
                                          start=(dc == 0), stop=(dc == 7))
                            cP.inc(PE.last)
                            SC.wait_ge(cP.h, cP.n)
                            SC.activation(hft[:, tb4 * HFF + cs * 512:tb4 * HFF + (cs + 1) * 512],
                                          bank[:, :], AF.Gelu, scale=0.5,
                                          accum_out=mus[:, tb4 * 8 + cs:tb4 * 8 + cs + 1])
                            SC.wait_ge(cA.h, cA.n)
                            SC.activation(sf1[:, :],
                                          hft[:, tb4 * HFF + cs * 512:tb4 * HFF + (cs + 1) * 512],
                                          AF.Square,
                                          accum_out=m2s[:, tb4 * 8 + cs:tb4 * 8 + cs + 1])
                    for tb4 in range(TBT):
                        tb = ti * TBT + tb4
                        sl = slice(tb * 128, (tb + 1) * 128)
                        VE.wait_ge(cA.h, cA.n)
                        dS.wait(VE)
                        VE.tensor_reduce(sm[:, 0:1], mus[:, tb4 * 8:(tb4 + 1) * 8], AX.X, ALU.add)
                        VE.tensor_reduce(sm[:, 1:2], m2s[:, tb4 * 8:(tb4 + 1) * 8], AX.X, ALU.add)
                        VE.tensor_scalar(sm[:, 2:3], sm[:, 0:1], 1.0 / HFF, None, ALU.mult)
                        VE.tensor_mul(sm[:, 3:4], sm[:, 2:3], sm[:, 2:3])
                        VE.scalar_tensor_tensor(sm[:, 4:5], sm[:, 1:2], 1.0 / HFF, sm[:, 3:4],
                                                ALU.mult, ALU.subtract)
                        SC.wait_ge(cV.h, cV.n)
                        SC.activation(sm[:, 5:6], sm[:, 4:5], AF.Sqrt, bias=epsln[:])
                        VE.wait_ge(cA.h, cA.n)
                        VE.reciprocal(sm[:, 6:7], sm[:, 5:6])
                        VE.tensor_scalar(hnb[:], hft[:, tb4 * HFF:(tb4 + 1) * HFF],
                                         sm[:, 2:3], sm[:, 6:7], ALU.subtract, ALU.mult)
                        trwave([(hnb[:, kc * 128:(kc + 1) * 128],
                                 hffnfm[:, kc * 512 + tb4 * 128:kc * 512 + (tb4 + 1) * 128])
                                for kc in range(32)])
                    dS.wait(SY)
                    for kcg in range(4):
                        wbuf = wp1 if kcg % 2 == 0 else wp2
                        SY.wait_ge(cP.h, cP.n)
                        dma(SY, dS, wbuf.ap().rearrange("p (c n) -> p c n", c=8),
                            sp_wqp.rearrange("(c p) n -> p c n", p=128)
                            [:, kcg * 8:(kcg + 1) * 8, :])
                        dS.wait(PE)
                        if kcg == 0:
                            PE.wait_ge(cV.h, cV.n)
                            PE.wait_ge(cA.h, cA.n)
                        for tb4 in range(TBT):
                            for k8 in range(8):
                                kc = kcg * 8 + k8
                                for half in range(2):
                                    PE.matmul(
                                        pb[tb4 * 2 + half][:, :],
                                        hffnfm[:, kc * 512 + tb4 * 128:kc * 512 + (tb4 + 1) * 128],
                                        wbuf[:, k8 * 1024 + half * 512:k8 * 1024 + (half + 1) * 512],
                                        start=(kc == 0), stop=False)
                        cP.inc(PE.last)
                    for tb4 in range(TBT):
                        tb = ti * TBT + tb4
                        sl = slice(tb * 128, (tb + 1) * 128)
                        for half in range(2):
                            PE.matmul(pb[tb4 * 2 + half][:, :], onesrow[:],
                                      lnbw_row[0:1, half * 512:(half + 1) * 512],
                                      start=False, stop=True)
                        cP.inc(PE.last)
                        SY.wait_ge(cV.h, cV.n)
                        dma(SY, dS, acb[:], sp_acc[sl, :])
                        VE.wait_ge(cP.h, cP.n)
                        dS.wait(VE)
                        for half in range(2):
                            hsl = slice(half * 512, (half + 1) * 512)
                            VE.scalar_tensor_tensor(outb[:, hsl], pb[tb4 * 2 + half][:, :],
                                                    0.5, acb[:, hsl], ALU.mult, ALU.add)
                        SY.wait_ge(cV.h, cV.n)
                        dma(SY, dS, out_hbm[sl, :], outb[:])
                        dS.wait(SY)
                        dS.wait(VE)
                fence()

    return nc


_PERM = None


def _nperm():
    p = np.arange(DS).reshape(DS // 2, 2).T.reshape(-1)
    return p


def _prep_inputs(inputs):
    f32 = np.float32
    import ml_dtypes
    bf16 = ml_dtypes.bfloat16
    W_in = np.asarray(inputs["L0_attn_W_in"], f32)
    perm = _nperm()
    cols = np.arange(MAINC)
    main_idx = np.concatenate([
        np.arange(0, 4096),
        np.concatenate([4096 + h * DS + perm for h in range(NH)]),
        np.concatenate([6144 + h * DS + perm for h in range(NH)]),
    ])
    w_main = np.ascontiguousarray(W_in[:, main_idx])
    w_aux = np.ascontiguousarray(W_in[:, 8192:])

    def rep(v, n=128):
        return np.ascontiguousarray(np.broadcast_to(np.asarray(v, f32)[None, :], (n, len(v))))

    n1w = rep(inputs["L0_attn_norm1_w"])
    n2w = rep(inputs["L0_attn_norm2_w"])
    bnw = rep(np.asarray(inputs["L0_attn_Bn_w"], f32)[perm]).astype(bf16)
    cnw = rep(np.asarray(inputs["L0_attn_Cn_w"], f32)[perm]).astype(bf16)
    bb = np.asarray(inputs["L0_attn_B_bias"], f32).reshape(NH, DS)[:, perm].reshape(-1)
    cb = np.asarray(inputs["L0_attn_C_bias"], f32).reshape(NH, DS)[:, perm].reshape(-1)
    bbias = rep(bb).astype(bf16)
    cbias = rep(cb).astype(bf16)
    dparam = rep(inputs["L0_attn_D_param"])
    dtb = np.asarray(inputs["L0_attn_dt_bias"], f32)
    alog = np.asarray(inputs["L0_attn_A_log"], f32)
    dtbias_tm = rep(dtb)
    dtbias_pk = np.ascontiguousarray(np.tile(dtb, 4)[:, None])
    alog_pk = np.ascontiguousarray(np.tile(alog, 4)[:, None])
    lnw = np.ascontiguousarray(np.asarray(inputs["L0_ffn_ln_w"], f32).reshape(32, 128).T)
    lnb16 = np.ascontiguousarray(
        np.asarray(inputs["L0_ffn_ln_b"], f32).reshape(32, 128).T).astype(bf16)
    tri = (np.tril(np.ones((128, 128), f32)).T * INV2PI).astype(f32)
    tri = np.ascontiguousarray(tri)
    tt = np.arange(512)
    mask01 = (tt % 64 != 0).astype(f32)
    c63 = (63 - (tt % 64)).astype(f32)
    fmc = np.zeros((128, 3 * 512), f32)
    fmc[:, 0:512] = mask01[None, :]
    fmc[:, 512:1024] = c63[None, :]
    k = np.arange(128)
    m = np.arange(128)
    shiftsum = (((k[:, None] % 32) == (m[None, :] % 32)) &
                ((k[:, None] // 32) < (m[None, :] // 32))).astype(f32)
    sel = np.zeros((128, 16 * 128), f32)
    for g in range(16):
        for mm in range(128):
            kk = 2 * g + mm // 64
            sel[np.arange(4) * 32 + kk, g * 128 + mm] = 1.0
    ident = np.eye(128, dtype=f32)

    com = dict(
        w_main=w_main, w_aux=w_aux,
        w_out=np.asarray(inputs["L0_attn_W_out"], f32),
        w_g=np.asarray(inputs["L0_attn_Wg"], f32),
        w_u=np.asarray(inputs["L0_attn_Wu"], f32),
        w_d=np.asarray(inputs["L0_attn_Wd"], f32),
        w_fct=np.ascontiguousarray(np.asarray(inputs["L0_ffn_W_fc"], f32).T),
        w_projt=np.ascontiguousarray(np.asarray(inputs["L0_ffn_W_proj"], f32).T),
        n1w=n1w, n2w=n2w, bnw=bnw, cnw=cnw, bbias=bbias, cbias=cbias,
        dparam=dparam, dtbias_tm=dtbias_tm, dtbias_pk=dtbias_pk,
        nalog_pk=alog_pk, alog_row=np.ascontiguousarray(alog[None, :]),
        fmc=fmc, shiftsum=shiftsum, lnw_c=lnw, lnb16_c=lnb16,
        tri_c=tri, sel_c=sel, identf_c=ident,
    )
    x = np.asarray(inputs["L0_attn_x"], f32)
    in_maps = []
    for core in range(8):
        b, half = core // 2, core % 2
        shard = np.ascontiguousarray(x[b, half * T:(half + 1) * T, :])
        mask = np.full((128, 1), float(half), f32)
        m = dict(com)
        m["x"] = shard
        m["mask_c"] = mask
        in_maps.append(m)
    return in_maps


_CACHED = {}


def kernel(**inputs):
    key = "nc"
    if key not in _CACHED:
        _CACHED[key] = build()
    nc = _CACHED[key]
    in_maps = _prep_inputs(inputs)
    res = run_bass_kernel_spmd(nc, in_maps, core_ids=list(range(8)))
    out = np.zeros((4, 4096, 1024), np.float32)
    for core in range(8):
        b, half = core // 2, core % 2
        out[b, half * T:(half + 1) * T, :] = res.results[core]["out"]
    return out
